# version 07
# baseline (speedup 1.0000x reference)
import numpy as np
import ml_dtypes
import concourse.bass as bass
import concourse.mybir as mybir
from concourse.bass_utils import run_bass_kernel_spmd

F32 = mybir.dt.float32
BF16 = mybir.dt.bfloat16
I32 = mybir.dt.int32
U8 = mybir.dt.uint8
AF = mybir.ActivationFunctionType
ALU = mybir.AluOpType
AX = mybir.AxisListType
BF16NP = ml_dtypes.bfloat16

T = 4096
DBGIDENT = False
D = 1024
H = 16
HP = 8
HID = 3072
NT = 32
CH1 = 1024
ENGS = ("sync", "scalar", "vector", "tensor", "gpsimd")
KB = 1024


def _bucket(d):
    import math
    NB, MD = 32, 128
    me = NB // 2
    if d < me:
        return d
    v = me + int(math.log(max(d, 1) / me) / math.log(MD / me) * (NB - me))
    return min(v, NB - 1)


def _host_consts():
    c = {}
    c["ident_bf"] = np.eye(128, dtype=np.float32).astype(BF16NP)
    c["ones_f32"] = np.ones((1, 128), dtype=np.float32)
    c["ones_bf"] = np.ones((1, 128), dtype=np.float32).astype(BF16NP)
    oh = np.zeros((32, 512), dtype=np.float32)
    for t in range(512):
        d = 383 - t
        if d >= 0:
            oh[_bucket(d), t] = 0.125
    c["ohrev"] = oh
    oc = np.zeros((32, 128), dtype=np.float32)
    oc[31, :] = 0.125
    c["oc"] = oc
    mk = np.zeros((128, 256), dtype=np.float32)
    for p in range(128):
        for g in range(129 + p, 256):
            mk[p, g] = -1e9
    c["maskc"] = mk
    idx = np.zeros((128, 16), dtype=np.int32)
    for p in range(128):
        for h in range(16):
            idx[p, h] = h * 512 + 255 - p
    c["bandidx"] = idx
    return c


class Prog:
    def __init__(self):
        self.ops = {e: [] for e in ENGS}
        self.cnt = {}
        self.producer = {}
        self.evinfo = {}

    def sem(self, name, producer=None):
        self.cnt.setdefault(name, 0)
        if producer:
            self.producer[name] = producer

    def op(self, eng, fn, waits=(), inc=None, amt=1):
        w = [x for x in waits if x is not None]
        ev = None
        if inc is not None:
            self.cnt[inc] += amt
            ev = (inc, self.cnt[inc])
            self.evinfo[ev] = (eng, len(self.ops[eng]))
        self.ops[eng].append((w, fn, inc, amt))
        return ev

    def gate(self, name):
        self.sem(name)
        for e in ENGS:
            self.ops[e].append((("GATE", name), None, None, 0))

    def emit(self, block, sems):
        def make(engname):
            def body(engine):
                satisfied = {}
                gate_level = {}
                for opidx, (w, fn, inc, amt) in enumerate(self.ops[engname]):
                    if isinstance(w, tuple) and len(w) == 2 and w[0] == "GATE":
                        name = w[1]
                        tgt = gate_level.get(name, 0) + 5
                        engine.sem_inc(sems[name], 1)
                        engine.wait_ge(sems[name], tgt)
                        gate_level[name] = tgt
                        continue
                    for (s, v) in w:
                        if self.producer.get(s) == engname:
                            info = self.evinfo.get((s, v))
                            if info is None or (opidx - info[1]) > 12:
                                continue
                        if satisfied.get(s, 0) >= v:
                            continue
                        engine.wait_ge(sems[s], v)
                        satisfied[s] = v
                    inst = fn(engine)
                    if inc is not None:
                        inst.then_inc(sems[inc], amt)
            return body
        block.sync(make("sync"))
        block.scalar(make("scalar"))
        block.vector(make("vector"))
        block.tensor(make("tensor"))
        block.gpsimd(make("gpsimd"))


def build_nc(debug=False):
    nc = bass.Bass("TRN2", target_bir_lowering=False)
    hc = _host_consts()

    t_x = nc.dram_tensor("x", [T, D], F32, kind="ExternalInput")
    t_wq = nc.dram_tensor("wq", [D, D], F32, kind="ExternalInput")
    t_wk = nc.dram_tensor("wk", [D, D], F32, kind="ExternalInput")
    t_wv = nc.dram_tensor("wv", [D, D], F32, kind="ExternalInput")
    t_wo = nc.dram_tensor("wo", [D, D], F32, kind="ExternalInput")
    t_bq = nc.dram_tensor("bq", [1, D], F32, kind="ExternalInput")
    t_bk = nc.dram_tensor("bk", [1, D], F32, kind="ExternalInput")
    t_bv = nc.dram_tensor("bv", [1, D], F32, kind="ExternalInput")
    t_bo = nc.dram_tensor("bo", [1, D], F32, kind="ExternalInput")
    t_rel = nc.dram_tensor("rel", [32, 16], F32, kind="ExternalInput")
    t_wg = nc.dram_tensor("wg", [HID, D], I32, kind="ExternalInput")
    t_wu = nc.dram_tensor("wu", [HID, D], I32, kind="ExternalInput")
    t_wd = nc.dram_tensor("wd", [D, HID], I32, kind="ExternalInput")
    t_out = nc.dram_tensor("out", [T, D], F32, kind="ExternalOutput")

    t_identbf = nc.inline_tensor(hc["ident_bf"], name="identbf")
    t_onesf = nc.inline_tensor(hc["ones_f32"], name="onesf")
    t_onesbf = nc.inline_tensor(hc["ones_bf"], name="onesbf")
    t_ohrev = nc.inline_tensor(hc["ohrev"], name="ohrev")
    t_oc = nc.inline_tensor(hc["oc"], name="octab")
    t_maskc = nc.inline_tensor(hc["maskc"], name="maskc")
    t_bidx = nc.inline_tensor(hc["bandidx"], name="bandidx")

    sk = "ExternalOutput" if debug else "Internal"
    t_g = nc.dram_tensor("gscr", [16, 512], F32, kind=sk)
    t_qs = nc.dram_tensor("qscr", [HP, 128, T], BF16, kind=sk)
    t_ks = nc.dram_tensor("kscr", [HP, 128, T], BF16, kind=sk)
    t_vs = nc.dram_tensor("vscr", [HP, 128, NT, 128], BF16, kind=sk)
    t_x1 = nc.dram_tensor("x1scr", [NT, 128, D], F32, kind=sk)
    t_wdscr = nc.dram_tensor("wdscr", [24, 128, 1024], BF16, kind="Internal")
    t_afm = nc.dram_tensor("afmscr", [128, HP * T], BF16, kind=sk) if debug else None
    t_dbd = nc.dram_tensor("dbdscr", [128, 16 * 256], BF16, kind=sk) if debug else None
    t_xnd = nc.dram_tensor("xnscr", [128, HP * T], BF16, kind=sk) if debug else None
    t_wgd = nc.dram_tensor("wgdscr", [128, 8 * HID], BF16, kind=sk) if debug else None
    t_xn1d = nc.dram_tensor("xn1scr", [128, 2048], BF16, kind=sk) if debug else None
    t_dq0 = nc.dram_tensor("dq0", [128, T], BF16, kind=sk) if debug else None
    t_dq1 = nc.dram_tensor("dq1", [128, T], BF16, kind=sk) if debug else None
    t_dv0 = nc.dram_tensor("dv0", [128, T], BF16, kind=sk) if debug else None
    t_dv1 = nc.dram_tensor("dv1", [128, T], BF16, kind=sk) if debug else None
    t_dk0 = nc.dram_tensor("dk0", [128, T], BF16, kind=sk) if debug else None
    t_dsm = nc.dram_tensor("dsm", [128, 128], F32, kind=sk) if debug else None
    t_dpA = nc.dram_tensor("dpA", [128, T], BF16, kind=sk) if debug else None
    t_dpB = nc.dram_tensor("dpB", [128, T], BF16, kind=sk) if debug else None
    t_dtA = nc.dram_tensor("dtA", [128, T], BF16, kind=sk) if debug else None
    t_dtB = nc.dram_tensor("dtB", [128, T], BF16, kind=sk) if debug else None
    t_ddg = nc.dram_tensor("ddg", [128, 2 * 2 * 6 * 128], BF16, kind=sk) if debug else None
    t_dsm2 = nc.dram_tensor("dsm2", [128, 128], F32, kind=sk) if debug else None
    t_hd = nc.dram_tensor("hscr", [128, 256], BF16, kind=sk) if debug else None
    t_wsd = nc.dram_tensor("wsscr", [128, 4096], BF16, kind=sk) if debug else None
    t_wud = nc.dram_tensor("wudscr", [128, 8 * HID], BF16, kind=sk) if debug else None


    arena = nc.alloc_sbuf_tensor("arena", [128, 190 * KB], U8)
    base_addr = nc.lookup_mloc(arena).addr

    def sb(name, shape, dtype, off):
        h = nc.virtual_sbuf_tensor(name, list(shape), dtype)
        nc._tensor(h.name, list(shape), dtype, type="SB")
        mloc = nc.lookup_mloc(h)
        mloc.allocated = True
        mloc.addr = base_addr + off
        return h

    sb_identbf = sb("s_identbf", [128, 128], BF16, 0)
    sb_onesf = sb("s_onesf", [1, 128], F32, 256)
    sb_onesbf = sb("s_onesbf", [1, 128], BF16, 768)
    sb_bq = sb("s_bqc", [128, 8], F32, 1024)
    sb_bk = sb("s_bkc", [128, 8], F32, 1056)
    sb_ch = sb("s_ch", [128, 16], F32, 1088)
    sb_sm = sb("s_smalls", [128, 128], F32, 1152)
    sb_ln = sb("s_lnsm", [128, 16], F32, 1664)
    sb_rel = sb("s_rel", [32, 16], F32, 1728)
    sb_bidx = sb("s_bidx", [128, 16], I32, 1792)
    sb_mask = sb("s_maskc", [128, 256], F32, 1856)
    sb_bvrow = sb("s_bvrow", [1, 1024], BF16, 2880)
    sb_borow = sb("s_borow", [1, 1024], BF16, 4928)
    sb_gsb = sb("s_gsb", [16, 512], F32, 6976)
    sb_dbandf = sb("s_dbandf", [128, 256], F32, 6976)
    sb_dband = sb("s_dband", [128, 16 * 256], BF16, 10 * KB)
    sb_woT = sb("s_woT", [128, 8 * 1024], BF16, 18 * KB)
    DE = 34 * KB
    sb_wstg = sb("s_wstg", [128, 8 * 1024], F32, DE)
    sb_wbf = sb("s_wbf", [128, 8 * 1024], BF16, DE + 32 * KB)
    sb_oh = sb("s_ohsb", [32, 512], F32, DE + 48 * KB)
    sb_ocs = sb("s_ocsb", [32, 128], F32, DE + 50 * KB)
    sb_brows = sb("s_brows", [1, 2, 1024], F32, DE + 51 * KB)
    sb_wqT = sb("s_wqT", [128, 8 * 1024], BF16, DE + 64 * KB)
    sb_wkT = sb("s_wkT", [128, 8 * 1024], BF16, DE + 80 * KB)
    sb_wvT = sb("s_wvT", [128, 8 * 1024], BF16, DE + 96 * KB)
    sb_xnfm = sb("s_xnfm", [128, 8 * T], BF16, DE)
    sb_afm = sb("s_afm", [128, 8 * T], BF16, DE)
    sb_q = sb("s_qstr", [128, 2, T], BF16, DE + 64 * KB)
    sb_k = sb("s_kstr", [128, 2, T], BF16, DE + 80 * KB)
    sb_v = sb("s_vstr", [128, 2, T], BF16, DE + 96 * KB)
    sb_wgT = sb("s_wgT", [128, 8 * HID], BF16, DE + 64 * KB)
    sb_wuT = sb("s_wuT", [128, 8 * HID], BF16, DE)
    sb_wdstr = sb("s_wdstr", [128, 2, 4 * 1024], BF16, DE + 48 * KB)
    sb_wdstage = sb("s_wdstage", [128, 8 * 128], BF16, 146 * KB + 16 * KB)
    F0 = 146 * KB
    sb_xt = sb("s_xt", [128, 2, D], F32, F0)
    sb_sq = sb("s_sqs", [128, D], F32, F0 + 8 * KB)
    sb_xntm = sb("s_xntm", [128, 2, D], BF16, F0 + 12 * KB)
    sb_qkst = sb("s_qkst", [128, 4, 512], BF16, F0 + 16 * KB)
    sb_vst = sb("s_vst", [128, 2, D], BF16, F0 + 20 * KB)
    sb_pA = sb("s_pA", [128, T], BF16, F0)
    sb_pB = sb("s_pB", [128, T], BF16, F0 + 8 * KB)
    sb_pTA = sb("s_pTA", [128, T], BF16, F0 + 16 * KB)
    sb_pTB = sb("s_pTB", [128, T], BF16, F0 + 24 * KB)
    sb_diag = sb("s_diag", [128, 2, 2, 6, 128], BF16, F0 + 32 * KB)
    sb_x1st = sb("s_x1st", [128, 2, D], F32, F0 + 8 * KB)
    sb_istg = sb("s_istg", [128, 2, D], I32, F0 + 16 * KB)
    sb_ibf = sb("s_ibf", [128, 2, D], BF16, F0 + 24 * KB)
    sb_x1t = sb("s_x1t", [128, 2, D], F32, F0)
    sb_xn1tm = sb("s_xn1tm", [128, 2, D], BF16, F0 + 8 * KB)
    sb_xn1fm = sb("s_xn1fm", [128, 2048], BF16, F0 + 12 * KB)
    sb_silu = sb("s_silu", [128, 2, 256], BF16, F0 + 28 * KB)
    sb_hsb = sb("s_hsb", [128, 2, 256], BF16, F0 + 29 * KB)
    sb_ost = sb("s_ost", [128, 2, D], F32, F0 + 30 * KB)
    sb_wdstg = sb("s_wdstg", [128, HID], I32, F0 + 38 * KB)
    sb_wdstg = sb("s_wdstg2", [128, HID], I32, F0 + 28 * KB)

    ps = nc.alloc_psum_tensor("ps", [128, 4096], F32)

    def psbf(c0, nblk):
        return ps[:, c0:c0 + nblk * 64].bitcast(BF16)

    P = Prog()
    for s, prod in (("pe", "tensor"), ("ve", "vector"), ("ac", "scalar"),
                    ("po", "gpsimd"), ("dS", None), ("dG", None)):
        P.sem(s, prod)
    PE, VE, AC, PO, DS, DG = "pe", "ve", "ac", "po", "dS", "dG"

    def dma(eng, dst, src, waits=()):
        q = DS if eng in ("sync", "scalar") else DG
        return P.op(eng, (lambda e, d=dst, s=src: e.dma_start(d, s)), waits=waits, inc=q, amt=16)

    e_rel = dma("sync", sb_rel[:, :], t_rel[:, :])
    e_id = dma("sync", sb_identbf[:, :], t_identbf[:, :])
    e_of = dma("sync", sb_onesf[:, :], t_onesf[:, :])
    e_ob = dma("sync", sb_onesbf[:, :], t_onesbf[:, :])
    e_mk = dma("sync", sb_mask[:, :], t_maskc[:, :])
    e_bi = dma("sync", sb_bidx[:, :], t_bidx[:, :])
    e_oh = dma("sync", sb_oh[:, :], t_ohrev[:, :])
    e_oc = dma("sync", sb_ocs[:, :], t_oc[:, :])
    e_bv = dma("sync", sb_brows[:, 0, :], t_bv[:, :])
    e_bo = dma("sync", sb_brows[:, 1, :], t_bo[:, :])
    def slow_col_dma(dst, t_row):
        src = t_row[:, :].rearrange("a (c p) -> (a p) c", p=128)
        return P.op("sync", (lambda e, d=dst, s=src: e.dma_start(
            d, s, allow_slow_non_contiguous=True)), inc=DS, amt=16)
    ev_bqc = slow_col_dma(sb_bq[:, :], t_bq)
    ev_bkc = slow_col_dma(sb_bk[:, :], t_bk)

    ev_g = P.op("tensor", lambda e: e.matmul(ps[0:16, 0:512], sb_rel[:, :], sb_oh[:, :],
                                             start=True, stop=True),
                waits=(e_rel, e_oh), inc=PE)
    ev_chm = P.op("tensor", lambda e: e.matmul(ps[0:128, 512:528], sb_ocs[:, :], sb_rel[:, :],
                                               start=True, stop=True),
                  waits=(e_rel, e_oc), inc=PE)
    ev_gc = P.op("vector", lambda e: e.tensor_copy(sb_gsb[:, :], ps[0:16, 0:512]),
                 waits=(ev_g,), inc=VE)
    ev_chc = P.op("vector", lambda e: e.tensor_copy(sb_ch[:, :], ps[0:128, 512:528]),
                  waits=(ev_chm,), inc=VE)
    ev_bvc = P.op("vector", lambda e: e.tensor_copy(sb_bvrow[:, :], sb_brows[:, 0, :]),
                  waits=(e_bv,), inc=VE)
    ev_boc = P.op("vector", lambda e: e.tensor_copy(sb_borow[:, :], sb_brows[:, 1, :]),
                  waits=(e_bo,), inc=VE)
    ev_gd = dma("sync", t_g[:, :], sb_gsb[:, :], waits=(ev_gc,))
    g_flat = t_g[:, :].flatten()[None, :]
    prev_band = None
    for h in range(16):
        def fg(e, h=h):
            return e.indirect_dma_start(sb_dbandf[:, :], None, g_flat,
                                        bass.IndirectOffsetOnAxis(sb_bidx[:, h:h + 1], 1))
        w_ = [ev_gd, e_bi]
        if prev_band is not None:
            w_.append(prev_band)
        ebg = P.op("gpsimd", fg, waits=tuple(w_), inc=DG, amt=16)
        def fs(e, h=h):
            return e.scalar_tensor_tensor(sb_dband[:, h * 256:(h + 1) * 256],
                                          sb_dbandf[:, :], sb_ch[:, h:h + 1],
                                          sb_mask[:, :], ALU.subtract, ALU.add)
        prev_band = P.op("vector", fs, waits=(ebg, ev_chc, e_mk), inc=VE)
    ev_band_done = prev_band

    P.gate("g0")
    wT_map = [(t_wq, sb_wqT), (t_wk, sb_wkT), (t_wv, sb_wvT), (t_wo, sb_woT)]
    parity_evac = {}
    prev_evac = None
    for wi, (t_w, dstT) in enumerate(wT_map):
        ld = dma("sync", sb_wstg[:, :], t_w[:, :].rearrange("(rc p) d -> p rc d", p=128),
                 waits=(prev_evac,) if prev_evac else ())
        cv = P.op("vector", lambda e: e.tensor_copy(sb_wbf[:, :], sb_wstg[:, :]),
                  waits=(ld,), inc=VE)
        for dc in range(8):
            gi = wi * 8 + dc
            par = gi % 2
            tr = None
            for rc in range(8):
                def f(e, dc=dc, rc=rc, par=par, st=(rc == 0), sp=(rc == 7)):
                    return e.matmul(psbf(3072 + par * 512, 8)[:, rc * 128:(rc + 1) * 128],
                                    sb_wbf[:, rc * 1024 + dc * 128: rc * 1024 + dc * 128 + 128],
                                    sb_identbf[:, :], is_transpose=True, start=st, stop=sp)
                w_ = [cv, e_id]
                if gi - 2 in parity_evac and rc == 0:
                    w_.append(parity_evac[gi - 2])
                tr = P.op("tensor", f, waits=tuple(w_), inc=PE if rc == 7 else None)
            def fe(e, dstT=dstT, dc=dc, par=par):
                return e.tensor_copy(dstT[:, dc * 1024:(dc + 1) * 1024], psbf(3072 + par * 512, 8))
            ev = P.op("vector", fe, waits=(tr,), inc=VE)
            parity_evac[gi] = ev
            prev_evac = ev
    ev_wprep = prev_evac

    P.gate("g1")
    LNC = 1.0 / D
    ev_xn = {}
    ev_xntm = {}
    ev_trl = {}
    xload = {}
    for tt in range(2):
        xload[tt] = dma("sync", sb_xt[:, tt % 2, :], t_x[tt * 128:(tt + 1) * 128, :])
    ev_var_prev = None
    for tt in range(NT):
        b = tt % 2
        xt = sb_xt[:, b, :]
        ev_s1 = P.op("vector", lambda e, xt=xt: e.tensor_reduce(sb_ln[:, 0:1], xt, AX.X, ALU.add),
                     waits=(xload[tt],), inc=VE)
        ev_s2 = P.op("scalar", lambda e, xt=xt: e.activation(sb_sq[:, :], xt, AF.Square,
                                                             accum_out=sb_ln[:, 1:2]),
                     waits=(xload[tt], ev_var_prev), inc=AC)
        ev_m = P.op("vector", lambda e: e.tensor_scalar_mul(sb_ln[:, 2:3], sb_ln[:, 0:1], LNC),
                    waits=(ev_s1,), inc=VE)
        ev_msq = P.op("vector", lambda e: e.tensor_scalar(sb_ln[:, 3:4], sb_ln[:, 2:3],
                                                          sb_ln[:, 2:3], 1e-5,
                                                          ALU.mult, ALU.subtract),
                      waits=(ev_m,), inc=VE)
        ev_var = P.op("vector", lambda e: e.scalar_tensor_tensor(sb_ln[:, 4:5], sb_ln[:, 1:2],
                                                                 LNC, sb_ln[:, 3:4],
                                                                 ALU.mult, ALU.subtract),
                      waits=(ev_s2, ev_msq), inc=VE)
        ev_var_prev = ev_var
        ev_lnv = P.op("scalar", lambda e: e.activation(sb_ln[:, 5:6], sb_ln[:, 4:5],
                                                       AF.Ln), waits=(ev_var,), inc=AC)
        ev_rstd = P.op("scalar", lambda e: e.activation(sb_ln[:, 6:7], sb_ln[:, 5:6],
                                                        AF.Exp, scale=-0.5),
                       waits=(ev_lnv,), inc=AC)
        def fx(e, xt=xt, b=b):
            return e.tensor_scalar(sb_xntm[:, b, :], xt, sb_ln[:, 2:3], sb_ln[:, 6:7],
                                   ALU.subtract, ALU.mult)
        ev_xntm[tt] = P.op("vector", fx, waits=(ev_rstd, ev_m, ev_trl.get(tt - 2)), inc=VE)
        tr = None
        for dc in range(8):
            def f(e, b=b, dc=dc, par=tt % 2, st=(dc == 0), sp=(dc == 7)):
                return e.matmul(psbf(3072 + par * 512, 8)[:, dc * 128:(dc + 1) * 128],
                                sb_xntm[:, b, dc * 128:(dc + 1) * 128], sb_identbf[:, :],
                                is_transpose=True, start=st, stop=sp)
            w_ = [ev_xntm[tt]]
            if dc == 0:
                w_.append(ev_xn.get(tt - 2))
            tr = P.op("tensor", f, waits=tuple(w_), inc=PE if dc == 7 else None)
        ev_trl[tt] = tr
        def fe(e, tt=tt, par=tt % 2):
            full = sb_xnfm[:, :].rearrange("p (dc t) -> p dc t", dc=8)
            return e.tensor_copy(full[:, :, tt * 128:(tt + 1) * 128],
                                 psbf(3072 + par * 512, 8).rearrange("p (dc t) -> p dc t", dc=8))
        ev_xn[tt] = P.op("vector", fe, waits=(tr,), inc=VE)
        if tt + 2 < NT:
            xload[tt + 2] = dma("sync", sb_xt[:, tt % 2, :],
                                t_x[(tt + 2) * 128:(tt + 3) * 128, :],
                                waits=(ev_xntm[tt],))
    ev_xn_all = ev_xn[NT - 1]

    stage_dma = {}
    bank_evac = {}
    ev_qk_last = None
    for which, (wT, bias_col, t_spill) in enumerate(
            ((sb_wqT, sb_bq, t_qs), (sb_wkT, sb_bk, t_ks))):
        for hp in range(HP):
            for grp, tcs in ((0, (0, 1, 2, 3, 4, 5)), (1, (6, 7))):
                mm = None
                for dc in range(8):
                    for tc in tcs:
                        bank = tc if grp == 0 else (tc - 6)
                        def f(e, dc=dc, tc=tc, hp=hp, wT=wT, bank=bank):
                            return e.matmul(ps[:, bank * 512:(bank + 1) * 512],
                                            wT[:, dc * 1024 + hp * 128: dc * 1024 + hp * 128 + 128],
                                            sb_xnfm[:, dc * T + tc * 512: dc * T + tc * 512 + 512],
                                            start=(dc == 0), stop=(dc == 7))
                        w_ = [ev_xn_all]
                        if dc == 0:
                            w_.append(bank_evac.get(bank))
                        mm = P.op("tensor", f, waits=tuple(w_), inc=PE if dc == 7 else None)
                for tc in tcs:
                    bank = tc if grp == 0 else (tc - 6)
                    slot = tc % 4
                    def fe(e, bank=bank, slot=slot, bias_col=bias_col, hp=hp):
                        return e.tensor_scalar_add(sb_qkst[:, slot, :],
                                                   ps[:, bank * 512:(bank + 1) * 512],
                                                   bias_col[:, hp:hp + 1])
                    w_ = [mm, stage_dma.get(slot), ev_bqc, ev_bkc]
                    eva = P.op("vector", fe, waits=tuple(w_), inc=VE)
                    bank_evac[bank] = eva
                    evd = dma("sync", t_spill[hp, :, tc * 512:(tc + 1) * 512],
                              sb_qkst[:, slot, :], waits=(eva,))
                    stage_dma[slot] = evd
                    ev_qk_last = evd
    ev_vsp = {}
    ev_vcp = {}
    for tt in range(NT):
        bank01 = (tt % 2) * 2
        mm = None
        for dc in range(8):
            for half in range(2):
                def f(e, tt=tt, dc=dc, half=half, bank01=bank01):
                    return e.matmul(ps[:, (bank01 + half) * 512:(bank01 + half + 1) * 512],
                                    sb_xnfm[:, dc * T + tt * 128: dc * T + tt * 128 + 128],
                                    sb_wvT[:, dc * 1024 + half * 512: dc * 1024 + half * 512 + 512],
                                    start=(dc == 0), stop=False)
                w_ = [ev_xn_all]
                if dc == 0 and half == 0:
                    w_.append(ev_vcp.get(tt - 2))
                    if tt < 2:
                        w_.append(bank_evac.get(bank01))
                        w_.append(bank_evac.get(bank01 + 1))
                mm = P.op("tensor", f, waits=tuple(w_), inc=None)
        for half in range(2):
            def fb(e, half=half, bank01=bank01):
                return e.matmul(ps[:, (bank01 + half) * 512:(bank01 + half + 1) * 512],
                                sb_onesbf[0:1, :], sb_bvrow[0:1, half * 512:(half + 1) * 512],
                                start=False, stop=True)
            mm = P.op("tensor", fb, waits=(ev_bvc, e_ob), inc=PE)
        b = tt % 2
        def fe(e, b=b, bank01=bank01):
            return e.tensor_copy(sb_vst[:, b, :], ps[:, bank01 * 512:(bank01 + 2) * 512])
        w_ = [mm, ev_vsp.get(tt - 2)]
        ev_vcp[tt] = P.op("vector", fe, waits=tuple(w_), inc=VE)
        evd = None
        for hp in range(HP):
            evd = dma("sync", t_vs[hp, :, tt, :], sb_vst[:, b, hp * 128:(hp + 1) * 128],
                      waits=(ev_vcp[tt],))
        ev_vsp[tt] = evd

    if debug:
        dma("sync", t_xnd[:, :], sb_xnfm[:, :], waits=(ev_xn_all,))

    P.gate("g2")
    CH = 1024
    ev_pv = {}
    ev_oev = {}
    ev_tr_last = {}
    ev_tev_ve = {}
    ev_tev_ac = {}
    ev_exp = {}
    ev_exp_last = {}
    ev_chain = {}
    ev_diag = {}
    ev_load = {}
    stage_last_evac = {}
    dbg_hold = [None]
    dbg_hold2 = [None]
    units = [(hp, it) for hp in range(HP) for it in range(NT)]
    slot_ctr = [0]
    slot_user = {}

    def emit_pair_load(hp, waits):
        b = hp % 2
        eq = dma("sync", sb_q[:, b, :], t_qs[hp, :, :], waits=waits)
        ek = dma("sync", sb_k[:, b, :], t_ks[hp, :, :], waits=waits)
        evv = dma("sync", sb_v[:, b, :], t_vs[hp, :, :, :], waits=waits)
        ev_load[hp] = (eq, ek, evv)

    for hp in range(2):
        emit_pair_load(hp, (ev_qk_last, ev_vsp[NT - 1]))

    def bank_spans(start, size):
        out = []
        cur = start
        end = start + size
        while cur < end:
            nxt = min(end, (cur // 512 + 1) * 512)
            out.append((cur, nxt - cur))
            cur = nxt
        return out

    def emit_tail(u):
        hp, it = u
        ui = units.index(u)
        b = hp % 2
        L = (it + 1) * 128
        njb = L // 128
        par_u = ui % 2
        prev = units[ui - 1] if ui > 0 else None
        tr = None
        last_ve = None
        last_ac = None
        nbat_total = 0
        for head in ("A", "B"):
            hx = 0 if head == "A" else 1
            pbuf = sb_pA if head == "A" else sb_pB
            pT = sb_pTA if head == "A" else sb_pTB
            for g0 in range(0, njb, 4):
                gn = min(4, njb - g0)
                par = nbat_total % 2
                base = 3072 + par * 512
                for j in range(gn):
                    jb = g0 + j
                    cidx = (jb * 128) // CH
                    def ft(e, pbuf=pbuf, jb=jb, j=j, base=base, par_u=par_u, hx=hx, cidx=cidx,
                           st=(j == 0), sp=(j == gn - 1)):
                        return e.matmul(ps[:, base + j * 128: base + (j + 1) * 128],
                                        pbuf[:, jb * 128:(jb + 1) * 128],
                                        sb_identbf[:, :] if DBGIDENT else sb_diag[:, par_u, hx, cidx, :],
                                        start=st, stop=sp)
                    w_ = []
                    if j == 0:
                        w_.append(stage_last_evac.get(par))
                        if par == 1 and prev is not None:
                            w_.append(ev_oev.get(prev))
                        if nbat_total == 0:
                            w_ = w_ + [ev_exp_last[u], ev_diag[u]]
                    tr = P.op("tensor", ft, waits=tuple(w_),
                              inc=PE if j == gn - 1 else None)
                def fev(e, pT=pT, g0=g0, gn=gn, base=base):
                    cp = getattr(e, "tensor_copy", None) or e.copy
                    return cp(pT[:, g0 * 128:(g0 + gn) * 128], ps[:, base:base + gn * 128])
                w_ = [tr]
                if prev:
                    w_.append(ev_pv[prev])
                eng = "vector" if (nbat_total % 2 == 0) else "scalar"
                evx = P.op(eng, fev, waits=tuple(w_), inc=VE if eng == "vector" else AC)
                stage_last_evac[par] = evx
                if eng == "vector":
                    last_ve = evx
                else:
                    last_ac = evx
                nbat_total += 1
        ev_tr_last[u] = tr
        ev_tev_ve[u] = last_ve or (VE, 0)
        ev_tev_ac[u] = last_ac or (AC, 0)
        pv = None
        for jb in range(njb):
            for head in ("A", "B"):
                coff = 0 if head == "A" else 64
                pT = sb_pTA if head == "A" else sb_pTB
                tp = (0, 0) if head == "A" else (0, 64)
                def fp(e, jb=jb, coff=coff, pT=pT, tp=tp, b=b,
                       first=(jb == 0), last=(jb == njb - 1)):
                    return e.matmul(ps[coff:coff + 64, 3584:3712],
                                    sb_v[:, b, jb * 128 + coff: jb * 128 + coff + 64],
                                    pT[:, jb * 128:(jb + 1) * 128],
                                    start=first, stop=last, tile_position=tp)
                w_ = []
                if jb == 0 and head == "A":
                    w_ = [ev_tev_ve[u], ev_tev_ac[u], ev_load[hp][2]]
                    if prev:
                        w_.append(ev_oev[prev])
                    if dbg_hold[0] is not None:
                        w_.append(dbg_hold[0])
                        dbg_hold[0] = None
                last_mm = (jb == njb - 1 and head == "B")
                pv = P.op("tensor", fp, waits=tuple(w_), inc=PE if last_mm else None)
        ev_pv[u] = pv
        if debug and u == (7, 31):
            d1 = dma("sync", t_dpA[:, :], sb_pA[:, :], waits=(ev_tev_ve[u], ev_tev_ac[u]))
            d2 = dma("sync", t_dpB[:, :], sb_pB[:, :], waits=(d1,))
            d3 = dma("sync", t_dtA[:, :], sb_pTA[:, :], waits=(d2,))
            d4 = dma("sync", t_dtB[:, :], sb_pTB[:, :], waits=(d3,))
            d5 = dma("sync", t_ddg[:, :], sb_diag[:, :, :, :, :].rearrange("p a b c d -> p (a b c d)"), waits=(d4,))
            d6 = dma("sync", t_dsm2[:, :], sb_sm[:, :], waits=(d5,))
        def fo(e, hp=hp, it=it):
            return e.tensor_copy(sb_afm[:, hp * T + it * 128: hp * T + (it + 1) * 128],
                                 ps[:, 3584:3712])
        ev_oev[u] = P.op("vector", fo, waits=(pv,), inc=VE)
        if it == NT - 1 and hp + 2 < HP:
            emit_pair_load(hp + 2, (ev_pv[u],))

    for ui, (hp, it) in enumerate(units):
        b = hp % 2
        L = (it + 1) * 128
        nch = (L + CH - 1) // CH
        prev = units[ui - 1] if ui > 0 else None
        prev2 = units[ui - 2] if ui > 1 else None
        sm0 = (ui % 2) * 64
        qA = sb_q[0:64, b, it * 128:(it + 1) * 128]
        qB = sb_q[64:128, b, it * 128:(it + 1) * 128]

        gl_blocks = [(L - 128, 128)] if L == 128 else [(L - 256, 0), (L - 128, 128)]

        unit_slot = {}
        for c in range(nch):
            j0 = c * CH
            s = min(CH, L - j0)
            dblocks = [(gc - j0, bc0) for (gc, bc0) in gl_blocks if gc // CH == c]
            qk_evs = {}
            for head in ("A", "B"):
                slot = (slot_ctr[0] % 3) * CH
                slot_ctr[0] += 1
                unit_slot[(c, head)] = slot
                w_slot = slot_user.get(slot)
                qh = qA if head == "A" else qB
                koff = 0 if head == "A" else 64
                tp = (0, 0) if head == "A" else (64, 0)
                dbanks = {}
                for (lc, bc0) in dblocks:
                    dbanks.setdefault((slot + lc) // 512, []).append((lc, bc0))
                spans = bank_spans(slot, s)
                n_d = len(dblocks)
                qk_ev = None
                for si, (off, n) in enumerate(spans):
                    has_d = (off // 512) in dbanks
                    is_last = (si == len(spans) - 1) and n_d == 0
                    def f(e, off=off, n=n, qh=qh, koff=koff, j0=j0, b=b, slot=slot,
                          has_d=has_d, tp=tp):
                        jrel = off - slot
                        return e.matmul(ps[:, off:off + n], qh,
                                        sb_k[koff:koff + 64, b, j0 + jrel: j0 + jrel + n],
                                        start=True, stop=not has_d, tile_position=tp)
                    w_ = []
                    if si == 0:
                        w_.append(w_slot)
                    if head == "A" and si == 0 and c == 0 and it == 0:
                        w_ += [ev_load[hp][0], ev_load[hp][1], ev_band_done]
                    qk_ev = P.op("tensor", f, waits=tuple(w_),
                                 inc=PE if is_last else None)
                di = 0
                for bkd, blist in sorted(dbanks.items()):
                    for (lc, bc0) in sorted(blist):
                        di += 1
                        last_in_bank = (lc == max(x for (x, _) in blist))
                        def f(e, slot=slot, lc=lc, bc0=bc0, hp=hp, koff=koff, last=last_in_bank):
                            h = 2 * hp + (0 if koff == 0 else 1)
                            return e.matmul(ps[:, slot + lc: slot + lc + 128],
                                            sb_identbf[:, :],
                                            sb_dband[:, h * 256 + bc0: h * 256 + bc0 + 128],
                                            start=False, stop=last)
                        qk_ev = P.op("tensor", f, waits=(),
                                     inc=PE if di == n_d else None)
                qk_evs[head] = qk_ev
            chain_evs = {}
            for head in ("A", "B"):
                qk_ev = qk_evs[head]
                rcol = sm0 + (0 if head == "A" else 32)
                slot = unit_slot[(c, head)]
                def fr(e, slot=slot, s=s, rcol=rcol, c=c):
                    return e.tensor_reduce(sb_sm[:, rcol + 6 + c:rcol + 7 + c],
                                           ps[:, slot:slot + s], AX.X, ALU.max, negate=True)
                w_ = [qk_ev]
                if c == 0 and head == "A" and prev2:
                    w_.append(ev_diag[prev2])
                    w_.append(ev_exp_last[prev2])
                ev_r = P.op("vector", fr, waits=tuple(w_), inc=VE)
                if c == 0:
                    def fc(e, rcol=rcol):
                        return e.tensor_copy(sb_sm[:, rcol:rcol + 1], sb_sm[:, rcol + 6:rcol + 7])
                else:
                    def fc(e, rcol=rcol, c=c):
                        return e.tensor_scalar(sb_sm[:, rcol + c:rcol + c + 1],
                                               sb_sm[:, rcol + 6 + c:rcol + 7 + c],
                                               sb_sm[:, rcol + c - 1:rcol + c], None,
                                               ALU.min)
                ev_r2 = P.op("gpsimd", fc, waits=(ev_r, ev_chain.get((hp, it, c - 1, head))), inc=PO)
                ev_chain[(hp, it, c, head)] = ev_r2
            if c == 0 and prev is not None:
                emit_tail(prev)
            for head in ("A", "B"):
                rcol = sm0 + (0 if head == "A" else 32)
                slot = unit_slot[(c, head)]
                ev_r2 = ev_chain[(hp, it, c, head)]
                pbuf = sb_pA if head == "A" else sb_pB
                def fxp(e, slot=slot, s=s, j0=j0, rcol=rcol, c=c, pbuf=pbuf):
                    return e.activation(pbuf[:, j0:j0 + s], ps[:, slot:slot + s], AF.Exp,
                                        bias=sb_sm[:, rcol + c:rcol + c + 1],
                                        accum_out=sb_sm[:, rcol + 12 + c:rcol + 13 + c])
                w_ = [ev_r2]
                if c == 0 and head == "A" and prev:
                    w_.append(ev_tr_last[prev])
                ev_x = P.op("scalar", fxp, waits=tuple(w_), inc=AC)
                ev_exp[(hp, it, c, head)] = ev_x
                slot_user[slot] = ev_x
        ev_exp_last[(hp, it)] = ev_exp[(hp, it, nch - 1, "B")]

        dg = None
        for head in ("A", "B"):
            hx = 0 if head == "A" else 1
            rcol = sm0 + (0 if head == "A" else 32)
            if nch == 1:
                ev_d = P.op("vector", lambda e, rcol=rcol: e.reciprocal(
                    sb_sm[:, rcol + 24:rcol + 25], sb_sm[:, rcol + 12:rcol + 13]),
                    waits=(ev_exp[(hp, it, 0, head)],), inc=VE)
            else:
                def f1(e, rcol=rcol, nch=nch):
                    return e.tensor_scalar(sb_sm[:, rcol + 18:rcol + 18 + nch],
                                           sb_sm[:, rcol:rcol + nch],
                                           sb_sm[:, rcol + nch - 1:rcol + nch], None,
                                           ALU.subtract)
                ev_f1 = P.op("gpsimd", f1,
                             waits=(ev_chain[(hp, it, nch - 1, head)],), inc=PO)
                def f2(e, rcol=rcol, nch=nch):
                    return e.activation(sb_sm[:, rcol + 24:rcol + 24 + nch],
                                        sb_sm[:, rcol + 18:rcol + 18 + nch], AF.Exp, scale=-1.0)
                ev_f2 = P.op("scalar", f2, waits=(ev_f1,), inc=AC)
                sf_evs = []
                for c in range(nch):
                    def f3(e, rcol=rcol, c=c):
                        return e.tensor_scalar_mul(sb_sm[:, rcol + 18 + c:rcol + 19 + c],
                                                   sb_sm[:, rcol + 12 + c:rcol + 13 + c],
                                                   sb_sm[:, rcol + 24 + c:rcol + 25 + c])
                    sf_evs.append(P.op("gpsimd", f3,
                                       waits=(ev_f2, ev_exp[(hp, it, c, head)]), inc=PO))
                def tsa(dst, a, bcol, rcol=rcol):
                    def f(e, dst=dst, a=a, bcol=bcol, rcol=rcol):
                        return e.tensor_scalar_add(sb_sm[:, rcol + dst:rcol + dst + 1],
                                                   sb_sm[:, rcol + a:rcol + a + 1],
                                                   sb_sm[:, rcol + bcol:rcol + bcol + 1])
                    return f
                if nch == 2:
                    ev_f4 = P.op("gpsimd", tsa(30, 18, 19), waits=tuple(sf_evs), inc=PO)
                elif nch == 3:
                    e22 = P.op("gpsimd", tsa(22, 18, 19), waits=tuple(sf_evs[:2]), inc=PO)
                    ev_f4 = P.op("gpsimd", tsa(30, 20, 22), waits=(sf_evs[2], e22), inc=PO)
                else:
                    e22 = P.op("gpsimd", tsa(22, 18, 19), waits=tuple(sf_evs[:2]), inc=PO)
                    e23 = P.op("gpsimd", tsa(23, 20, 21), waits=tuple(sf_evs[2:]), inc=PO)
                    ev_f4 = P.op("gpsimd", tsa(30, 22, 23), waits=(e22, e23), inc=PO)
                ev_rs = P.op("vector", lambda e, rcol=rcol: e.reciprocal(
                    sb_sm[:, rcol + 31:rcol + 32], sb_sm[:, rcol + 30:rcol + 31]),
                    waits=(ev_f4,), inc=VE)
                def f5(e, rcol=rcol, nch=nch):
                    return e.tensor_scalar_mul(sb_sm[:, rcol + 24:rcol + 24 + nch],
                                               sb_sm[:, rcol + 24:rcol + 24 + nch],
                                               sb_sm[:, rcol + 31:rcol + 32])
                ev_d = P.op("gpsimd", f5, waits=(ev_rs,), inc=PO)
            for c in range(nch):
                def fdg(e, hx=hx, c=c, rcol=rcol, par=ui % 2):
                    return e.tensor_scalar_mul(sb_diag[:, par, hx, c, :], sb_identbf[:, :],
                                               sb_sm[:, rcol + 24 + c:rcol + 25 + c])
                w_ = [ev_d]
                if prev2 is not None and c == 0 and head == "A":
                    w_.append(ev_tr_last.get(prev2))
                dg = P.op("vector", fdg, waits=tuple(w_), inc=VE)
        ev_diag[(hp, it)] = dg
        if ui == len(units) - 1:
            emit_tail((hp, it))
    if debug:
        dma("sync", t_afm[:, :], sb_afm[:, :])
        dma("sync", t_dbd[:, :], sb_dband[:, :])
        dma("sync", t_dq0[:, :], sb_q[:, 0, :])
        dma("sync", t_dq1[:, :], sb_q[:, 1, :])
        dma("sync", t_dv0[:, :], sb_v[:, 0, :])
        dma("sync", t_dv1[:, :], sb_v[:, 1, :])
        dma("sync", t_dk0[:, :], sb_k[:, 0, :])
        dma("sync", t_dsm[:, :], sb_sm[:, :])
    ev_x1 = {}
    ev_add = {}
    xl2 = {}
    for it in range(2):
        xl2[it] = dma("sync", sb_xt[:, it % 2, :], t_x[it * 128:(it + 1) * 128, :],
                      waits=(ev_tr_last[(HP - 1, NT - 1)],))
    for it in range(NT):
        b = it % 2
        bank01 = (it % 2) * 2
        mm = None
        for hpc in range(8):
            for half in range(2):
                def f(e, it=it, hpc=hpc, half=half, bank01=bank01):
                    return e.matmul(ps[:, (bank01 + half) * 512:(bank01 + half + 1) * 512],
                                    sb_afm[:, hpc * T + it * 128: hpc * T + (it + 1) * 128],
                                    sb_woT[:, hpc * 1024 + half * 512: hpc * 1024 + half * 512 + 512],
                                    start=(hpc == 0), stop=False)
                w_ = []
                if hpc == 0 and half == 0:
                    w_.append(ev_add.get(it - 2))
                mm = P.op("tensor", f, waits=tuple(w_), inc=None)
        for half in range(2):
            def fb(e, half=half, bank01=bank01):
                return e.matmul(ps[:, (bank01 + half) * 512:(bank01 + half + 1) * 512],
                                sb_onesbf[0:1, :], sb_borow[0:1, half * 512:(half + 1) * 512],
                                start=False, stop=True)
            mm = P.op("tensor", fb, waits=(ev_boc,), inc=PE)
        def fx(e, b=b, bank01=bank01):
            return e.tensor_add(sb_x1st[:, b, :], ps[:, bank01 * 512:(bank01 + 2) * 512],
                                sb_xt[:, b, :])
        ev_add[it] = P.op("vector", fx, waits=(mm, xl2[it], ev_x1.get(it - 2)), inc=VE)
        ev_x1[it] = dma("sync", t_x1[it, :, :], sb_x1st[:, b, :], waits=(ev_add[it],))
        if it + 2 < NT:
            xl2[it + 2] = dma("sync", sb_xt[:, it % 2, :],
                              t_x[(it + 2) * 128:(it + 3) * 128, :],
                              waits=(ev_add[it],))

    def ffn_weight_prep(t_w, dstT, nrc, after=()):
        prev_ev = {}
        for rc in range(nrc):
            bi = rc % 2
            w_ = list(after) if rc == 0 else []
            w_.append(prev_ev.get(rc - 2))
            ld = dma("gpsimd", sb_istg[:, bi, :], t_w[rc * 128:(rc + 1) * 128, :],
                     waits=tuple(w_))
            cv = P.op("vector", lambda e, bi=bi: e.tensor_scalar_mul(
                sb_ibf[:, bi, :], sb_istg[:, bi, :], 1.0 / 127.0), waits=(ld,), inc=VE)
            tr = None
            for dc in range(8):
                def f(e, bi=bi, dc=dc, par=rc % 2, st=(dc == 0), sp=(dc == 7)):
                    return e.matmul(psbf(3072 + par * 512, 8)[:, dc * 128:(dc + 1) * 128],
                                    sb_ibf[:, bi, dc * 128:(dc + 1) * 128], sb_identbf[:, :],
                                    is_transpose=True, start=st, stop=sp)
                w2 = [cv]
                if dc == 0:
                    w2.append(prev_ev.get(rc - 2))
                tr = P.op("tensor", f, waits=tuple(w2), inc=PE if dc == 7 else None)
            def fe(e, rc=rc, par=rc % 2, dstT=dstT):
                full = dstT[:, :].rearrange("p (dc h) -> p dc h", dc=8)
                return e.tensor_copy(full[:, :, rc * 128:(rc + 1) * 128],
                                     psbf(3072 + par * 512, 8).rearrange("p (dc t) -> p dc t", dc=8))
            ev = P.op("vector", fe, waits=(tr,), inc=VE)
            prev_ev[rc] = ev
        return prev_ev[nrc - 1]

    ev_wg = ffn_weight_prep(t_wg, sb_wgT, 24, after=(ev_pv[(HP - 1, NT - 1)],))

    P.gate("g4")
    ev_wu = ffn_weight_prep(t_wu, sb_wuT, 24, after=())
    prev_wd = {}
    last_wd = None
    for rc2 in range(8):
        w_ = [ev_wu] if rc2 == 0 else [prev_wd.get(rc2 - 1)]
        ld = dma("gpsimd", sb_wdstg[:, :], t_wd[rc2 * 128:(rc2 + 1) * 128, :], waits=tuple(w_))
        evc = {}
        for chk in range(3):
            bi = chk % 2
            def fcv(e, chk=chk, bi=bi):
                return e.tensor_scalar_mul(sb_ibf[:, bi, :],
                                           sb_wdstg[:, chk * 1024:(chk + 1) * 1024], 1.0 / 127.0)
            w2 = [ld, evc.get(chk - 2)]
            if rc2 == 0 and chk < 2:
                w2.append(ev_wu)
            cv = P.op("vector", fcv, waits=tuple(w2), inc=VE)
            tr = None
            for q in range(8):
                def f(e, bi=bi, q=q, par=(rc2 + chk) % 2, st=(q == 0), sp=(q == 7)):
                    return e.matmul(psbf(3072 + par * 512, 8)[:, q * 128:(q + 1) * 128],
                                    sb_ibf[:, bi, q * 128:(q + 1) * 128], sb_identbf[:, :],
                                    is_transpose=True, start=st, stop=sp)
                w3 = [cv]
                if q == 0:
                    w3.append(prev_wd.get((rc2, chk - 2)) or prev_wd.get((rc2 - 1, chk + 1)))
                tr = P.op("tensor", f, waits=tuple(w3), inc=PE if q == 7 else None)
            def fe(e, rc2=rc2, chk=chk, par=(rc2 + chk) % 2):
                return e.tensor_copy(sb_wdstage[:, :], psbf(3072 + par * 512, 8))
            w_st = [tr]
            if (rc2, chk, "dma") in prev_wd:
                pass
            w_st.append(prev_wd.get("stage"))
            ev = P.op("vector", fe, waits=tuple(w_st), inc=VE)
            evd = None
            for q in range(8):
                hcq = chk * 8 + q
                evd = dma("sync", t_wdscr[hcq, :, rc2 * 128:(rc2 + 1) * 128],
                          sb_wdstage[:, q * 128:(q + 1) * 128], waits=(ev,))
            prev_wd["stage"] = evd
            evc[chk] = ev
            prev_wd[(rc2, chk)] = ev
        prev_wd[rc2] = evc[2]
        last_wd = prev_wd["stage"]

    ev_ffn = {}
    ev_odma = {}
    ev_fm = {}
    ev_tm2 = {}
    ev_var_prev = None
    ev_add3 = {}
    dwn_by_group = {}
    for ts in range(16):
        for k in range(2):
            tt = 2 * ts + k
            w_ = [last_wd, ev_x1[NT - 1]] if ts == 0 else [ev_ffn.get(ts - 1)]
            ld = dma("sync", sb_x1t[:, k, :], t_x1[tt, :, :], waits=tuple(w_))
            x1t = sb_x1t[:, k, :]
            ev_s1 = P.op("vector", lambda e, x1t=x1t: e.tensor_reduce(
                sb_ln[:, 0:1], x1t, AX.X, ALU.add), waits=(ld,), inc=VE)
            ev_s2 = P.op("scalar", lambda e, x1t=x1t: e.activation(
                sb_sq[:, :], x1t, AF.Square, accum_out=sb_ln[:, 1:2]),
                waits=(ld, ev_var_prev), inc=AC)
            ev_m = P.op("vector", lambda e: e.tensor_scalar_mul(sb_ln[:, 2:3], sb_ln[:, 0:1], LNC),
                        waits=(ev_s1,), inc=VE)
            ev_msq = P.op("vector", lambda e: e.tensor_scalar(
                sb_ln[:, 3:4], sb_ln[:, 2:3], sb_ln[:, 2:3], 1e-5,
                ALU.mult, ALU.subtract),
                waits=(ev_m,), inc=VE)
            ev_var = P.op("vector", lambda e: e.scalar_tensor_tensor(
                sb_ln[:, 4:5], sb_ln[:, 1:2], LNC, sb_ln[:, 3:4], ALU.mult, ALU.subtract),
                waits=(ev_s2, ev_msq), inc=VE)
            ev_var_prev = ev_var
            ev_lnv = P.op("scalar", lambda e: e.activation(
                sb_ln[:, 5:6], sb_ln[:, 4:5], AF.Ln), waits=(ev_var,), inc=AC)
            ev_rstd = P.op("scalar", lambda e: e.activation(
                sb_ln[:, 6:7], sb_ln[:, 5:6], AF.Exp, scale=-0.5),
                waits=(ev_lnv,), inc=AC)
            def ftm(e, k=k, x1t=x1t):
                return e.tensor_scalar(sb_xn1tm[:, k, :], x1t, sb_ln[:, 2:3], sb_ln[:, 6:7],
                                       ALU.subtract, ALU.mult)
            ev_tm2[(ts, k)] = P.op("vector", ftm, waits=(ev_rstd, ev_m, ev_tm2.get((ts - 1, k))), inc=VE)
            tr = None
            for dc in range(8):
                def f(e, k=k, dc=dc, par=k, st=(dc == 0), sp=(dc == 7)):
                    return e.matmul(psbf(3072 + par * 512, 8)[:, dc * 128:(dc + 1) * 128],
                                    sb_xn1tm[:, k, dc * 128:(dc + 1) * 128], sb_identbf[:, :],
                                    is_transpose=True, start=st, stop=sp)
                w2 = [ev_tm2[(ts, k)]]
                if dc == 0:
                    w2.append(ev_fm.get((ts - 1, k)))
                    w2.append(ev_add3.get((ts - 1, 1)))
                tr = P.op("tensor", f, waits=tuple(w2), inc=PE if dc == 7 else None)
            def fev(e, k=k, par=k):
                full = sb_xn1fm[:, :].rearrange("p (dc t) -> p dc t", t=256)
                return e.tensor_copy(full[:, :, k * 128:(k + 1) * 128],
                                     psbf(3072 + par * 512, 8).rearrange("p (dc t) -> p dc t", dc=8))
            ev_fm[(ts, k)] = P.op("vector", fev, waits=(tr,), inc=VE)

        hprev = {}
        down_last = None
        wd_ld = {}
        def emit_wd_load(hg, ts=ts):
            wb = hg % 2
            prev_use = dwn_by_group.get((ts, hg - 2))
            if prev_use is None and ts > 0:
                prev_use = dwn_by_group.get((ts - 1, hg + 4 - 6 + 2))
            if prev_use is None and ts > 0:
                prev_use = ev_ffn.get(ts - 1)
            wd_ld[hg] = dma("sync", sb_wdstr[:, wb, :],
                            t_wdscr[hg * 4:(hg + 1) * 4, :, :].rearrange("h p d -> p h d"),
                            waits=(prev_use, last_wd))
        for hg in range(2):
            emit_wd_load(hg)
        for hh in range(24):
            gslot = (hh % 2) * 1024
            uslot = 512 + (hh % 2) * 1024
            bsl = hh % 2
            mmg = None
            mmu = None
            for dc in range(8):
                def fg(e, hh=hh, dc=dc, gslot=gslot):
                    return e.matmul(ps[:, gslot:gslot + 256],
                                    sb_wgT[:, dc * HID + hh * 128: dc * HID + hh * 128 + 128],
                                    sb_xn1fm[:, dc * 256:(dc + 1) * 256],
                                    start=(dc == 0), stop=(dc == 7))
                w2 = []
                if dc == 0:
                    w2 = [ev_fm[(ts, 0)], ev_fm[(ts, 1)], hprev.get(hh - 2)]
                mmg = P.op("tensor", fg, waits=tuple(w2), inc=PE if dc == 7 else None)
                def fu(e, hh=hh, dc=dc, uslot=uslot):
                    return e.matmul(ps[:, uslot:uslot + 256],
                                    sb_wuT[:, dc * HID + hh * 128: dc * HID + hh * 128 + 128],
                                    sb_xn1fm[:, dc * 256:(dc + 1) * 256],
                                    start=(dc == 0), stop=(dc == 7))
                mmu = P.op("tensor", fu, waits=(), inc=PE if dc == 7 else None)
            def fs(e, gslot=gslot, bsl=bsl):
                return e.activation(sb_silu[:, bsl, :], ps[:, gslot:gslot + 256], AF.Silu)
            ev_si = P.op("scalar", fs, waits=(mmg, hprev.get(hh - 2)), inc=AC)
            def fm2(e, uslot=uslot, bsl=bsl):
                return e.tensor_mul(sb_hsb[:, bsl, :], ps[:, uslot:uslot + 256],
                                    sb_silu[:, bsl, :])
            ev_h = P.op("vector", fm2, waits=(ev_si, mmu), inc=VE)
            hprev[hh] = ev_h
            if debug and ts == 0 and hh == 0:
                dma("sync", t_xn1d[:, :], sb_xn1fm[:, :], waits=(ev_fm[(ts, 0)], ev_fm[(ts, 1)]))
                dma("sync", t_hd[:, :], sb_hsb[:, bsl, :], waits=(ev_h,))
                dma("sync", t_wsd[:, :], sb_wdstr[:, 0, :], waits=(wd_ld[0],))
            for tsub in range(2):
                for half in range(2):
                    def fd(e, hh=hh, tsub=tsub, half=half, bsl=bsl,
                           first=(hh == 0), last=(hh == 23)):
                        c0 = 2048 + tsub * 1024 + half * 512
                        wb = (hh // 4) % 2
                        woff = (hh % 4) * 1024 + half * 512
                        return e.matmul(ps[:, c0:c0 + 512],
                                        sb_hsb[:, bsl, tsub * 128:(tsub + 1) * 128],
                                        sb_wdstr[:, wb, woff:woff + 512],
                                        start=first, stop=last)
                    w2 = [ev_h]
                    if hh % 4 == 0 and tsub == 0 and half == 0:
                        w2.append(wd_ld[hh // 4])
                    if hh == 0 and tsub == 0 and half == 0 and ts >= 1:
                        w2.append(ev_ffn.get(ts - 1))
                    down_last = P.op("tensor", fd, waits=tuple(w2),
                                     inc=PE if (tsub == 1 and half == 1) else None)
            if hh % 4 == 3:
                dwn_by_group[(ts, hh // 4)] = down_last
                if hh // 4 + 2 < 6:
                    emit_wd_load(hh // 4 + 2)
        evo = None
        for tsub in range(2):
            def fo2(e, tsub=tsub):
                return e.tensor_add(sb_ost[:, tsub, :],
                                    ps[:, 2048 + tsub * 1024: 2048 + (tsub + 1) * 1024],
                                    sb_x1t[:, tsub, :])
            w2 = [down_last, ev_odma.get((ts - 1, tsub))]
            ev_a = P.op("vector", fo2, waits=tuple(w2), inc=VE)
            ev_add3[(ts, tsub)] = ev_a
            tt = 2 * ts + tsub
            evo = dma("sync", t_out[tt * 128:(tt + 1) * 128, :], sb_ost[:, tsub, :],
                      waits=(ev_a,))
            ev_odma[(ts, tsub)] = evo
        ev_ffn[ts] = evo

    if debug:
        dma("sync", t_wgd[:, :], sb_wgT[:, :], waits=(ev_ffn[15],))
        dma("sync", t_wud[:, :], sb_wuT[:, :], waits=(ev_ffn[15],))

    final_ds = P.cnt[DS]
    P.op("sync", lambda e: e.nop(), waits=((DS, final_ds), (DG, P.cnt[DG])), inc=None)

    sems = {}
    import contextlib
    ctx = contextlib.ExitStack()
    for name in list(P.cnt.keys()):
        sems[name] = ctx.enter_context(nc.semaphore(name))
    with nc.Block() as block:
        P.emit(block, sems)
    ctx.close()
    return nc


_NC_CACHE = None


def _get_nc():
    global _NC_CACHE
    if _NC_CACHE is None:
        _NC_CACHE = build_nc()
    return _NC_CACHE


def kernel(L0_attn_x, L0_attn_Wq, L0_attn_bq, L0_attn_Wk, L0_attn_bk,
           L0_attn_Wv, L0_attn_bv, L0_attn_Wo, L0_attn_bo, L0_attn_rel_emb,
           L0_ffn_w_gate, L0_ffn_w_up, L0_ffn_w_down):
    nc = _get_nc()
    x = np.ascontiguousarray(np.asarray(L0_attn_x, dtype=np.float32))
    com = {
        "wq": np.ascontiguousarray(np.asarray(L0_attn_Wq, dtype=np.float32)),
        "wk": np.ascontiguousarray(np.asarray(L0_attn_Wk, dtype=np.float32)),
        "wv": np.ascontiguousarray(np.asarray(L0_attn_Wv, dtype=np.float32)),
        "wo": np.ascontiguousarray(np.asarray(L0_attn_Wo, dtype=np.float32)),
        "bq": np.ascontiguousarray(np.asarray(L0_attn_bq, dtype=np.float32).reshape(1, D)),
        "bk": np.ascontiguousarray(np.asarray(L0_attn_bk, dtype=np.float32).reshape(1, D)),
        "bv": np.ascontiguousarray(np.asarray(L0_attn_bv, dtype=np.float32).reshape(1, D)),
        "bo": np.ascontiguousarray(np.asarray(L0_attn_bo, dtype=np.float32).reshape(1, D)),
        "rel": np.ascontiguousarray(np.asarray(L0_attn_rel_emb, dtype=np.float32)),
        "wg": np.ascontiguousarray(np.asarray(L0_ffn_w_gate, dtype=np.int32)),
        "wu": np.ascontiguousarray(np.asarray(L0_ffn_w_up, dtype=np.int32)),
        "wd": np.ascontiguousarray(np.asarray(L0_ffn_w_down, dtype=np.int32)),
    }
    in_maps = []
    for c in range(8):
        m = dict(com)
        m["x"] = np.ascontiguousarray(x[c])
        in_maps.append(m)
    res = run_bass_kernel_spmd(nc, in_maps, core_ids=list(range(8)))
    out = np.stack([np.asarray(r["out"], dtype=np.float32) for r in res.results], axis=0)
    return out


# version 18 (final)
# speedup vs baseline: 1.5538x; 1.5538x over previous
import numpy as np
import ml_dtypes
import concourse.bass as bass
import concourse.mybir as mybir
from concourse.bass_utils import run_bass_kernel_spmd

F32 = mybir.dt.float32
BF16 = mybir.dt.bfloat16
I32 = mybir.dt.int32
U8 = mybir.dt.uint8
AF = mybir.ActivationFunctionType
ALU = mybir.AluOpType
AX = mybir.AxisListType
BF16NP = ml_dtypes.bfloat16

T = 4096
DBGIDENT = False
D = 1024
H = 16
HP = 8
HID = 3072
NT = 32
CH1 = 1024
ENGS = ("sync", "scalar", "vector", "tensor", "gpsimd")
KB = 1024


def _bucket(d):
    import math
    NB, MD = 32, 128
    me = NB // 2
    if d < me:
        return d
    v = me + int(math.log(max(d, 1) / me) / math.log(MD / me) * (NB - me))
    return min(v, NB - 1)


def _host_consts():
    c = {}
    c["ident_bf"] = np.eye(128, dtype=np.float32).astype(BF16NP)
    c["ones_f32"] = np.ones((1, 128), dtype=np.float32)
    c["ones_bf"] = np.ones((1, 128), dtype=np.float32).astype(BF16NP)
    oh = np.zeros((32, 512), dtype=np.float32)
    for t in range(512):
        d = 383 - t
        if d >= 0:
            oh[_bucket(d), t] = 0.125
    c["ohrev"] = oh
    oc = np.zeros((32, 128), dtype=np.float32)
    oc[31, :] = 0.125
    c["oc"] = oc
    mk = np.zeros((128, 256), dtype=np.float32)
    for p in range(128):
        for g in range(129 + p, 256):
            mk[p, g] = -1e9
    c["maskc"] = mk
    idx = np.zeros((128, 16), dtype=np.int32)
    for p in range(128):
        for h in range(16):
            idx[p, h] = h * 512 + 255 - p
    c["bandidx"] = idx
    return c


class Prog:
    def __init__(self):
        self.ops = {e: [] for e in ENGS}
        self.cnt = {}
        self.producer = {}
        self.evinfo = {}

    def sem(self, name, producer=None):
        self.cnt.setdefault(name, 0)
        if producer:
            self.producer[name] = producer

    def op(self, eng, fn, waits=(), inc=None, amt=1):
        w = [x for x in waits if x is not None]
        ev = None
        if inc is not None:
            self.cnt[inc] += amt
            ev = (inc, self.cnt[inc])
            self.evinfo[ev] = (eng, len(self.ops[eng]))
        self.ops[eng].append((w, fn, inc, amt))
        return ev

    def gate(self, name):
        self.sem(name)
        for e in ENGS:
            self.ops[e].append((("GATE", name), None, None, 0))

    def emit(self, block, sems):
        def make(engname):
            def body(engine):
                satisfied = {}
                gate_level = {}
                for opidx, (w, fn, inc, amt) in enumerate(self.ops[engname]):
                    if isinstance(w, tuple) and len(w) == 2 and w[0] == "GATE":
                        name = w[1]
                        tgt = gate_level.get(name, 0) + 5
                        engine.sem_inc(sems[name], 1)
                        engine.wait_ge(sems[name], tgt)
                        gate_level[name] = tgt
                        continue
                    for (s, v) in w:
                        if self.producer.get(s) == engname:
                            info = self.evinfo.get((s, v))
                            if info is None or (opidx - info[1]) > 12:
                                continue
                        if satisfied.get(s, 0) >= v:
                            continue
                        engine.wait_ge(sems[s], v)
                        satisfied[s] = v
                    inst = fn(engine)
                    if inc is not None:
                        inst.then_inc(sems[inc], amt)
            return body
        block.sync(make("sync"))
        block.scalar(make("scalar"))
        block.vector(make("vector"))
        block.tensor(make("tensor"))
        block.gpsimd(make("gpsimd"))


def build_nc(debug=False):
    nc = bass.Bass("TRN2", target_bir_lowering=False)
    hc = _host_consts()

    t_x = nc.dram_tensor("x", [T, D], F32, kind="ExternalInput")
    t_wq = nc.dram_tensor("wq", [D, D], F32, kind="ExternalInput")
    t_wk = nc.dram_tensor("wk", [D, D], F32, kind="ExternalInput")
    t_wv = nc.dram_tensor("wv", [D, D], F32, kind="ExternalInput")
    t_wo = nc.dram_tensor("wo", [D, D], F32, kind="ExternalInput")
    t_bq = nc.dram_tensor("bq", [1, D], F32, kind="ExternalInput")
    t_bk = nc.dram_tensor("bk", [1, D], F32, kind="ExternalInput")
    t_bv = nc.dram_tensor("bv", [1, D], F32, kind="ExternalInput")
    t_bo = nc.dram_tensor("bo", [1, D], F32, kind="ExternalInput")
    t_rel = nc.dram_tensor("rel", [32, 16], F32, kind="ExternalInput")
    t_wg = nc.dram_tensor("wg", [HID, D], I32, kind="ExternalInput")
    t_wu = nc.dram_tensor("wu", [HID, D], I32, kind="ExternalInput")
    t_wd = nc.dram_tensor("wd", [D, HID], I32, kind="ExternalInput")
    t_out = nc.dram_tensor("out", [T, D], F32, kind="ExternalOutput")

    t_identbf = nc.inline_tensor(hc["ident_bf"], name="identbf")
    t_onesf = nc.inline_tensor(hc["ones_f32"], name="onesf")
    t_onesbf = nc.inline_tensor(hc["ones_bf"], name="onesbf")
    t_ohrev = nc.inline_tensor(hc["ohrev"], name="ohrev")
    t_oc = nc.inline_tensor(hc["oc"], name="octab")
    t_maskc = nc.inline_tensor(hc["maskc"], name="maskc")
    t_bidx = nc.inline_tensor(hc["bandidx"], name="bandidx")

    sk = "ExternalOutput" if debug else "Internal"
    t_g = nc.dram_tensor("gscr", [16, 512], F32, kind=sk)
    t_qs = nc.dram_tensor("qscr", [HP, 128, T], BF16, kind=sk)
    t_ks = nc.dram_tensor("kscr", [HP, 128, T], BF16, kind=sk)
    t_vs = nc.dram_tensor("vscr", [HP, 128, NT, 128], BF16, kind=sk)
    t_x1 = nc.dram_tensor("x1scr", [NT, 128, D], F32, kind=sk)
    t_wdscr = nc.dram_tensor("wdscr", [24, 128, 1024], BF16, kind="Internal")
    t_afm = nc.dram_tensor("afmscr", [128, HP * T], BF16, kind=sk) if debug else None
    t_dbd = nc.dram_tensor("dbdscr", [128, 16 * 256], BF16, kind=sk) if debug else None
    t_xnd = nc.dram_tensor("xnscr", [128, HP * T], BF16, kind=sk) if debug else None
    t_wgd = nc.dram_tensor("wgdscr", [128, 8 * HID], BF16, kind=sk) if debug else None
    t_xn1d = nc.dram_tensor("xn1scr", [128, 2048], BF16, kind=sk) if debug else None
    t_dq0 = nc.dram_tensor("dq0", [128, T], BF16, kind=sk) if debug else None
    t_dq1 = nc.dram_tensor("dq1", [128, T], BF16, kind=sk) if debug else None
    t_dv0 = nc.dram_tensor("dv0", [128, T], BF16, kind=sk) if debug else None
    t_dv1 = nc.dram_tensor("dv1", [128, T], BF16, kind=sk) if debug else None
    t_dk0 = nc.dram_tensor("dk0", [128, T], BF16, kind=sk) if debug else None
    t_dsm = nc.dram_tensor("dsm", [128, 128], F32, kind=sk) if debug else None
    t_hd = nc.dram_tensor("hscr", [128, 256], BF16, kind=sk) if debug else None
    t_wsd = nc.dram_tensor("wsscr", [128, 4096], BF16, kind=sk) if debug else None
    t_wud = nc.dram_tensor("wudscr", [128, 8 * HID], BF16, kind=sk) if debug else None


    arena = nc.alloc_sbuf_tensor("arena", [128, 190 * KB], U8)
    base_addr = nc.lookup_mloc(arena).addr

    def sb(name, shape, dtype, off):
        h = nc.virtual_sbuf_tensor(name, list(shape), dtype)
        nc._tensor(h.name, list(shape), dtype, type="SB")
        mloc = nc.lookup_mloc(h)
        mloc.allocated = True
        mloc.addr = base_addr + off
        return h

    sb_identbf = sb("s_identbf", [128, 128], BF16, 0)
    sb_onesf = sb("s_onesf", [1, 128], F32, 256)
    sb_onesbf = sb("s_onesbf", [1, 128], BF16, 768)
    sb_bq = sb("s_bqc", [128, 8], F32, 1024)
    sb_bk = sb("s_bkc", [128, 8], F32, 1056)
    sb_ch = sb("s_ch", [128, 16], F32, 1088)
    sb_sm = sb("s_smalls", [128, 128], F32, 1152)
    sb_ln = sb("s_lnsm", [128, 16], F32, 1664)
    sb_rel = sb("s_rel", [32, 16], F32, 1728)
    sb_bidx = sb("s_bidx", [128, 16], I32, 1792)
    sb_mask = sb("s_maskc", [128, 256], F32, 1856)
    sb_bvrow = sb("s_bvrow", [1, 1024], BF16, 2880)
    sb_borow = sb("s_borow", [1, 1024], BF16, 4928)
    sb_gsb = sb("s_gsb", [16, 512], F32, 6976)
    sb_dbandf = sb("s_dbandf", [128, 2, 256], F32, 6976)
    sb_dband = sb("s_dband", [128, 16 * 256], BF16, 10 * KB)
    sb_woT = sb("s_woT", [128, 8 * 1024], BF16, 18 * KB)
    DE = 34 * KB
    sb_wstg = sb("s_wstg", [128, 8 * 1024], F32, DE)
    sb_wbf = sb("s_wbf", [128, 8 * 1024], BF16, DE + 32 * KB)
    sb_oh = sb("s_ohsb", [32, 512], F32, DE + 48 * KB)
    sb_ocs = sb("s_ocsb", [32, 128], F32, DE + 50 * KB)
    sb_brows = sb("s_brows", [1, 2, 1024], F32, DE + 51 * KB)
    sb_wqT = sb("s_wqT", [128, 8 * 1024], BF16, DE + 64 * KB)
    sb_wkT = sb("s_wkT", [128, 8 * 1024], BF16, DE + 80 * KB)
    sb_wvT = sb("s_wvT", [128, 8 * 1024], BF16, DE + 96 * KB)
    sb_xnfm = sb("s_xnfm", [128, 8 * T], BF16, DE)
    sb_afm = sb("s_afm", [128, 8 * T], BF16, DE)
    sb_q = sb("s_qstr", [128, 2, T], BF16, DE + 64 * KB)
    sb_k = sb("s_kstr", [128, 2, T], BF16, DE + 80 * KB)
    sb_v = sb("s_vstr", [128, 2, T], BF16, DE + 96 * KB)
    sb_wgT = sb("s_wgT", [128, 8 * HID], BF16, DE + 64 * KB)
    sb_wuT = sb("s_wuT", [128, 8 * HID], BF16, DE)
    sb_wdstr = sb("s_wdstr", [128, 2, 4 * 1024], BF16, DE + 48 * KB)
    sb_wdstage = sb("s_wdstage", [128, 8 * 128], BF16, 146 * KB + 16 * KB)
    F0 = 146 * KB
    sb_xt = sb("s_xt", [128, 2, D], F32, F0)
    sb_sq = sb("s_sqs", [128, D], F32, F0 + 8 * KB)
    sb_xntm = sb("s_xntm", [128, 2, D], BF16, F0 + 12 * KB)
    sb_qkst = sb("s_qkst", [128, 4, 512], BF16, F0 + 16 * KB)
    sb_vst = sb("s_vst", [128, 2, D], BF16, F0 + 20 * KB)
    sb_pA = sb("s_pA", [128, 5 * 1024], BF16, F0)
    sb_pB = sb("s_pB", [128, 5 * 1024], BF16, F0 + 10 * KB)
    sb_pTA = sb("s_pTA", [128, T], BF16, F0 + 20 * KB)
    sb_pTB = sb("s_pTB", [128, T], BF16, F0 + 28 * KB)
    sb_x1st = sb("s_x1st", [128, 2, D], F32, F0 + 8 * KB)
    sb_istg = sb("s_istg", [128, 2, D], I32, F0 + 16 * KB)
    sb_ibf = sb("s_ibf", [128, 2, D], BF16, F0 + 24 * KB)
    sb_x1t = sb("s_x1t", [128, 2, D], F32, F0)
    sb_xn1tm = sb("s_xn1tm", [128, 2, D], BF16, F0 + 8 * KB)
    sb_xn1fm = sb("s_xn1fm", [128, 2048], BF16, F0 + 12 * KB)
    sb_silu = sb("s_silu", [128, 2, 256], BF16, F0 + 28 * KB)
    sb_hsb = sb("s_hsb", [128, 2, 256], BF16, F0 + 29 * KB)
    sb_ost = sb("s_ost", [128, 2, D], F32, F0 + 30 * KB)
    sb_wdstg = sb("s_wdstg", [128, HID], I32, F0 + 38 * KB)
    sb_wdstg = sb("s_wdstg2", [128, HID], I32, F0 + 28 * KB)

    ps = nc.alloc_psum_tensor("ps", [128, 4096], F32)

    def psbf(c0, nblk):
        return ps[:, c0:c0 + nblk * 64].bitcast(BF16)

    P = Prog()
    for s, prod in (("pe", "tensor"), ("ve", "vector"), ("ac", "scalar"),
                    ("po", "gpsimd"), ("dS", None), ("dG", None)):
        P.sem(s, prod)
    PE, VE, AC, PO, DS, DG = "pe", "ve", "ac", "po", "dS", "dG"

    def dma(eng, dst, src, waits=()):
        q = DS if eng in ("sync", "scalar") else DG
        return P.op(eng, (lambda e, d=dst, s=src: e.dma_start(d, s)), waits=waits, inc=q, amt=16)

    e_rel = dma("sync", sb_rel[:, :], t_rel[:, :])
    e_id = dma("sync", sb_identbf[:, :], t_identbf[:, :])
    e_of = dma("sync", sb_onesf[:, :], t_onesf[:, :])
    e_ob = dma("sync", sb_onesbf[:, :], t_onesbf[:, :])
    e_mk = dma("sync", sb_mask[:, :], t_maskc[:, :])
    e_bi = dma("sync", sb_bidx[:, :], t_bidx[:, :])
    e_oh = dma("sync", sb_oh[:, :], t_ohrev[:, :])
    e_oc = dma("sync", sb_ocs[:, :], t_oc[:, :])
    e_bv = dma("sync", sb_brows[:, 0, :], t_bv[:, :])
    e_bo = dma("sync", sb_brows[:, 1, :], t_bo[:, :])
    def slow_col_dma(dst, t_row):
        src = t_row[:, :].rearrange("a (c p) -> (a p) c", p=128)
        return P.op("sync", (lambda e, d=dst, s=src: e.dma_start(
            d, s, allow_slow_non_contiguous=True)), inc=DS, amt=16)
    ev_bqc = slow_col_dma(sb_bq[:, :], t_bq)
    ev_bkc = slow_col_dma(sb_bk[:, :], t_bk)

    ev_g = P.op("tensor", lambda e: e.matmul(ps[0:16, 0:512], sb_rel[:, :], sb_oh[:, :],
                                             start=True, stop=True),
                waits=(e_rel, e_oh), inc=PE)
    ev_chm = P.op("tensor", lambda e: e.matmul(ps[0:128, 512:528], sb_ocs[:, :], sb_rel[:, :],
                                               start=True, stop=True),
                  waits=(e_rel, e_oc), inc=PE)
    ev_gc = P.op("vector", lambda e: e.tensor_copy(sb_gsb[:, :], ps[0:16, 0:512]),
                 waits=(ev_g,), inc=VE)
    ev_chc = P.op("vector", lambda e: e.tensor_copy(sb_ch[:, :], ps[0:128, 512:528]),
                  waits=(ev_chm,), inc=VE)
    ev_bvc = P.op("vector", lambda e: e.tensor_copy(sb_bvrow[:, :], sb_brows[:, 0, :]),
                  waits=(e_bv,), inc=VE)
    ev_boc = P.op("vector", lambda e: e.tensor_copy(sb_borow[:, :], sb_brows[:, 1, :]),
                  waits=(e_bo,), inc=VE)
    ev_gd = dma("sync", t_g[:, :], sb_gsb[:, :], waits=(ev_gc,))
    g_flat = t_g[:, :].flatten()[None, :]
    band_stt = {}
    prev_band = None
    for h in range(16):
        def fg(e, h=h):
            return e.indirect_dma_start(sb_dbandf[:, h % 2, :], None, g_flat,
                                        bass.IndirectOffsetOnAxis(sb_bidx[:, h:h + 1], 1))
        w_ = [ev_gd, e_bi, band_stt.get(h - 2)]
        ebg = P.op("gpsimd", fg, waits=tuple(w_), inc=DG, amt=16)
        def fs(e, h=h):
            return e.scalar_tensor_tensor(sb_dband[:, h * 256:(h + 1) * 256],
                                          sb_dbandf[:, h % 2, :], sb_ch[:, h:h + 1],
                                          sb_mask[:, :], ALU.subtract, ALU.add)
        prev_band = P.op("vector", fs, waits=(ebg, ev_chc, e_mk), inc=VE)
        band_stt[h] = prev_band
    ev_band_done = prev_band

    wT_map = [(t_wq, sb_wqT), (t_wk, sb_wkT), (t_wv, sb_wvT), (t_wo, sb_woT)]
    parity_evac = {}
    prev_evac = None
    prev_cv = None
    prev_trl = None
    for wi, (t_w, dstT) in enumerate(wT_map):
        ld = dma("sync", sb_wstg[:, :], t_w[:, :].rearrange("(rc p) d -> p rc d", p=128),
                 waits=(prev_cv,) if prev_cv else ())
        cv = P.op("vector", lambda e: e.tensor_copy(sb_wbf[:, :], sb_wstg[:, :]),
                  waits=(ld, prev_trl), inc=VE)
        prev_cv = cv
        for dc in range(8):
            gi = wi * 8 + dc
            par = gi % 2
            tr = None
            for rc in range(8):
                def f(e, dc=dc, rc=rc, par=par, st=(rc == 0), sp=(rc == 7)):
                    return e.matmul(psbf(3072 + par * 512, 8)[:, rc * 128:(rc + 1) * 128],
                                    sb_wbf[:, rc * 1024 + dc * 128: rc * 1024 + dc * 128 + 128],
                                    sb_identbf[:, :], is_transpose=True, start=st, stop=sp)
                w_ = [cv, e_id]
                if gi - 2 in parity_evac and rc == 0:
                    w_.append(parity_evac[gi - 2])
                tr = P.op("tensor", f, waits=tuple(w_), inc=PE if rc == 7 else None)
            def fe(e, dstT=dstT, dc=dc, par=par):
                return e.tensor_copy(dstT[:, dc * 1024:(dc + 1) * 1024], psbf(3072 + par * 512, 8))
            ev = P.op("vector", fe, waits=(tr,), inc=VE)
            parity_evac[gi] = ev
            prev_evac = ev
            if dc == 7:
                prev_trl = tr
    ev_wprep = prev_evac

    P.gate("g1")
    LNC = 1.0 / D
    ev_xn = {}
    ev_xntm = {}
    ev_trl = {}
    xload = {}
    for tt in range(2):
        xload[tt] = dma("sync", sb_xt[:, tt % 2, :], t_x[tt * 128:(tt + 1) * 128, :])
    def emit_ln_head(tt):
        b = tt % 2
        base = b * 8
        xt = sb_xt[:, b, :]
        ev_s1 = P.op("vector", lambda e, xt=xt, base=base: e.tensor_reduce(
            sb_ln[:, base:base + 1], xt, AX.X, ALU.add), waits=(xload[tt],), inc=VE)
        ev_s2 = P.op("scalar", lambda e, xt=xt, base=base: e.activation(
            sb_sq[:, :], xt, AF.Square, accum_out=sb_ln[:, base + 1:base + 2]),
            waits=(xload[tt],), inc=AC)
        ev_m = P.op("vector", lambda e, base=base: e.tensor_scalar_mul(
            sb_ln[:, base + 2:base + 3], sb_ln[:, base:base + 1], LNC),
            waits=(ev_s1,), inc=VE)
        ev_msq = P.op("vector", lambda e, base=base: e.tensor_scalar(
            sb_ln[:, base + 3:base + 4], sb_ln[:, base + 2:base + 3],
            sb_ln[:, base + 2:base + 3], 1e-5, ALU.mult, ALU.subtract),
            waits=(ev_m,), inc=VE)
        ev_var = P.op("vector", lambda e, base=base: e.scalar_tensor_tensor(
            sb_ln[:, base + 4:base + 5], sb_ln[:, base + 1:base + 2], LNC,
            sb_ln[:, base + 3:base + 4], ALU.mult, ALU.subtract),
            waits=(ev_s2, ev_msq), inc=VE)
        ev_lnv = P.op("scalar", lambda e, base=base: e.activation(
            sb_ln[:, base + 5:base + 6], sb_ln[:, base + 4:base + 5], AF.Ln),
            waits=(ev_var,), inc=AC)
        ev_rstd = P.op("scalar", lambda e, base=base: e.activation(
            sb_ln[:, base + 6:base + 7], sb_ln[:, base + 5:base + 6], AF.Exp, scale=-0.5),
            waits=(ev_lnv,), inc=AC)
        return ev_rstd

    ln_head = {}
    for tt in range(2):
        ln_head[tt] = emit_ln_head(tt)
    for tt in range(NT):
        b = tt % 2
        base = b * 8
        xt = sb_xt[:, b, :]
        def fx(e, xt=xt, b=b, base=base):
            return e.tensor_scalar(sb_xntm[:, b, :], xt, sb_ln[:, base + 2:base + 3],
                                   sb_ln[:, base + 6:base + 7],
                                   ALU.subtract, ALU.mult)
        ev_xntm[tt] = P.op("vector", fx, waits=(ln_head[tt], ev_trl.get(tt - 2)), inc=VE)
        tr = None
        for dc in range(8):
            def f(e, b=b, dc=dc, par=tt % 2, st=(dc == 0), sp=(dc == 7)):
                return e.matmul(psbf(3072 + par * 512, 8)[:, dc * 128:(dc + 1) * 128],
                                sb_xntm[:, b, dc * 128:(dc + 1) * 128], sb_identbf[:, :],
                                is_transpose=True, start=st, stop=sp)
            w_ = [ev_xntm[tt]]
            if dc == 0:
                w_.append(ev_xn.get(tt - 2))
            tr = P.op("tensor", f, waits=tuple(w_), inc=PE if dc == 7 else None)
        ev_trl[tt] = tr
        def fe(e, tt=tt, par=tt % 2):
            full = sb_xnfm[:, :].rearrange("p (dc t) -> p dc t", dc=8)
            return e.tensor_copy(full[:, :, tt * 128:(tt + 1) * 128],
                                 psbf(3072 + par * 512, 8).rearrange("p (dc t) -> p dc t", dc=8))
        ev_xn[tt] = P.op("vector", fe, waits=(tr,), inc=VE)
        if tt + 2 < NT:
            xload[tt + 2] = dma("sync", sb_xt[:, tt % 2, :],
                                t_x[(tt + 2) * 128:(tt + 3) * 128, :],
                                waits=(ev_xntm[tt],))
            ln_head[tt + 2] = emit_ln_head(tt + 2)
    ev_xn_all = ev_xn[NT - 1]

    stage_dma = {}
    bank_evac = {}
    ev_qk_last = None
    for which, (wT, bias_col, t_spill) in enumerate(
            ((sb_wqT, sb_bq, t_qs), (sb_wkT, sb_bk, t_ks))):
        for hp in range(HP):
            for grp, tcs in ((0, (0, 1, 2, 3, 4, 5)), (1, (6, 7))):
                mm = None
                for dc in range(8):
                    for tc in tcs:
                        bank = tc if grp == 0 else (tc - 6)
                        def f(e, dc=dc, tc=tc, hp=hp, wT=wT, bank=bank):
                            return e.matmul(ps[:, bank * 512:(bank + 1) * 512],
                                            wT[:, dc * 1024 + hp * 128: dc * 1024 + hp * 128 + 128],
                                            sb_xnfm[:, dc * T + tc * 512: dc * T + tc * 512 + 512],
                                            start=(dc == 0), stop=(dc == 7))
                        w_ = [ev_xn_all]
                        if dc == 0:
                            w_.append(bank_evac.get(bank))
                        mm = P.op("tensor", f, waits=tuple(w_), inc=PE if dc == 7 else None)
                for tc in tcs:
                    bank = tc if grp == 0 else (tc - 6)
                    slot = tc % 4
                    def fe(e, bank=bank, slot=slot, bias_col=bias_col, hp=hp):
                        return e.tensor_scalar_add(sb_qkst[:, slot, :],
                                                   ps[:, bank * 512:(bank + 1) * 512],
                                                   bias_col[:, hp:hp + 1])
                    w_ = [mm, stage_dma.get(slot), ev_bqc, ev_bkc]
                    eva = P.op("vector", fe, waits=tuple(w_), inc=VE)
                    bank_evac[bank] = eva
                    evd = dma("sync", t_spill[hp, :, tc * 512:(tc + 1) * 512],
                              sb_qkst[:, slot, :], waits=(eva,))
                    stage_dma[slot] = evd
                    ev_qk_last = evd
    ev_vsp = {}
    ev_vcp = {}
    for tt in range(NT):
        bank01 = (tt % 2) * 2
        mm = None
        for dc in range(8):
            for half in range(2):
                def f(e, tt=tt, dc=dc, half=half, bank01=bank01):
                    return e.matmul(ps[:, (bank01 + half) * 512:(bank01 + half + 1) * 512],
                                    sb_xnfm[:, dc * T + tt * 128: dc * T + tt * 128 + 128],
                                    sb_wvT[:, dc * 1024 + half * 512: dc * 1024 + half * 512 + 512],
                                    start=(dc == 0), stop=False)
                w_ = [ev_xn_all]
                if dc == 0 and half == 0:
                    w_.append(ev_vcp.get(tt - 2))
                    if tt < 2:
                        w_.append(bank_evac.get(bank01))
                        w_.append(bank_evac.get(bank01 + 1))
                mm = P.op("tensor", f, waits=tuple(w_), inc=None)
        for half in range(2):
            def fb(e, half=half, bank01=bank01):
                return e.matmul(ps[:, (bank01 + half) * 512:(bank01 + half + 1) * 512],
                                sb_onesbf[0:1, :], sb_bvrow[0:1, half * 512:(half + 1) * 512],
                                start=False, stop=True)
            mm = P.op("tensor", fb, waits=(ev_bvc, e_ob), inc=PE)
        b = tt % 2
        def fe(e, b=b, bank01=bank01):
            return e.tensor_copy(sb_vst[:, b, :], ps[:, bank01 * 512:(bank01 + 2) * 512])
        w_ = [mm, ev_vsp.get(tt - 2)]
        ev_vcp[tt] = P.op("vector", fe, waits=tuple(w_), inc=VE)
        evd = None
        for hp in range(HP):
            evd = dma("sync", t_vs[hp, :, tt, :], sb_vst[:, b, hp * 128:(hp + 1) * 128],
                      waits=(ev_vcp[tt],))
        ev_vsp[tt] = evd

    if debug:
        dma("sync", t_xnd[:, :], sb_xnfm[:, :], waits=(ev_xn_all,))

    P.gate("g2")
    CH = 1024
    ev_pv = {}
    ev_oev = {}
    ev_tr_last = {}
    ev_tev_ve = {}
    ev_tev_ac = {}
    ev_exp = {}
    ev_exp_last = {}
    ev_m0 = {}
    ev_chain = {}
    ev_diag = {}
    ev_load = {}
    stage_last_evac = {}
    dbg_hold = [None]
    dbg_hold2 = [None]
    units = [(hp, it) for hp in range(HP) for it in range(NT)]
    slot_ctr = [0]
    slot_user = {}
    p_ctr = {"A": [0], "B": [0]}
    p_slot_user = {}
    unit_pslot = {}
    ev_resc = {}
    ev_resc_last = {}

    def emit_pair_load(hp, waits):
        b = hp % 2
        eq = dma("sync", sb_q[:, b, :], t_qs[hp, :, :], waits=waits)
        ek = dma("sync", sb_k[:, b, :], t_ks[hp, :, :], waits=waits)
        evv = dma("sync", sb_v[:, b, :], t_vs[hp, :, :, :], waits=waits)
        ev_load[hp] = (eq, ek, evv)

    for hp in range(2):
        emit_pair_load(hp, (ev_qk_last, ev_vsp[NT - 1]))

    def bank_spans(start, size):
        out = []
        cur = start
        end = start + size
        while cur < end:
            nxt = min(end, (cur // 512 + 1) * 512)
            out.append((cur, nxt - cur))
            cur = nxt
        return out

    def emit_tail(u):
        hp, it = u
        ui = units.index(u)
        b = hp % 2
        L = (it + 1) * 128
        njb = L // 128
        nch_u = (L + CH - 1) // CH
        prev = units[ui - 1] if ui > 0 else None
        tr = None
        last_ve = None
        last_ac = None
        nbat_total = 0
        for head in ("A", "B"):
            pbuf = sb_pA if head == "A" else sb_pB
            pT = sb_pTA if head == "A" else sb_pTB
            for cc in range(nch_u):
                g0 = cc * 8
                gn = min(8, njb - g0)
                sloc = unit_pslot[(u, cc, head)]
                par = nbat_total % 2
                trb = None
                for j in range(gn):
                    def ft(e, pbuf=pbuf, sloc=sloc, j=j, par=par,
                           st=(j == 0), sp=(j == gn - 1)):
                        return e.matmul(psbf(3072 + par * 512, 8)[:, j * 128:(j + 1) * 128],
                                        pbuf[:, sloc * CH + j * 128: sloc * CH + (j + 1) * 128],
                                        sb_identbf[:, :], is_transpose=True,
                                        start=st, stop=sp)
                    w_ = []
                    if j == 0:
                        w_.append(stage_last_evac.get(par))
                        if par == 1 and prev is not None:
                            w_.append(ev_oev.get(prev))
                        w_.append(ev_resc[(u, cc, head)])
                    trb = P.op("tensor", ft, waits=tuple(w_),
                               inc=PE if j == gn - 1 else None)
                tr = trb
                p_slot_user[(head, sloc)] = trb
                def fev(e, pT=pT, cc=cc, gn=gn, par=par):
                    cp = getattr(e, "tensor_copy", None) or e.copy
                    return cp(pT[:, cc * CH: cc * CH + gn * 128],
                              psbf(3072 + par * 512, 8)[:, 0:gn * 128])
                w_ = [trb]
                if prev:
                    w_.append(ev_pv[prev])
                eng = "vector" if (nbat_total % 2 == 0) else "scalar"
                evx = P.op(eng, fev, waits=tuple(w_), inc=VE if eng == "vector" else AC)
                stage_last_evac[par] = evx
                if eng == "vector":
                    last_ve = evx
                else:
                    last_ac = evx
                nbat_total += 1
        ev_tr_last[u] = tr
        ev_tev_ve[u] = last_ve or (VE, 0)
        ev_tev_ac[u] = last_ac or (AC, 0)
        pv = None
        for jb in range(njb):
            for head in ("A", "B"):
                coff = 0 if head == "A" else 64
                pT = sb_pTA if head == "A" else sb_pTB
                tp = (0, 0) if head == "A" else (0, 64)
                def fp(e, jb=jb, coff=coff, pT=pT, tp=tp, b=b,
                       first=(jb == 0), last=(jb == njb - 1)):
                    return e.matmul(ps[coff:coff + 64, 3584:3712],
                                    sb_v[:, b, jb * 128 + coff: jb * 128 + coff + 64],
                                    pT[:, jb * 128:(jb + 1) * 128],
                                    start=first, stop=last, tile_position=tp)
                w_ = []
                if jb == 0 and head == "A":
                    w_ = [ev_tev_ve[u], ev_tev_ac[u], ev_load[hp][2]]
                    if prev:
                        w_.append(ev_oev[prev])
                    if dbg_hold[0] is not None:
                        w_.append(dbg_hold[0])
                        dbg_hold[0] = None
                last_mm = (jb == njb - 1 and head == "B")
                pv = P.op("tensor", fp, waits=tuple(w_), inc=PE if last_mm else None)
        ev_pv[u] = pv
        def fo(e, hp=hp, it=it):
            cp = getattr(e, "tensor_copy", None) or e.copy
            return cp(sb_afm[:, hp * T + it * 128: hp * T + (it + 1) * 128],
                      ps[:, 3584:3712])
        ev_oev[u] = P.op("scalar", fo, waits=(pv,), inc=AC)
        if it == NT - 1 and hp + 2 < HP:
            emit_pair_load(hp + 2, (ev_pv[u],))

    for ui, (hp, it) in enumerate(units):
        b = hp % 2
        L = (it + 1) * 128
        nch = (L + CH - 1) // CH
        prev = units[ui - 1] if ui > 0 else None
        prev2 = units[ui - 2] if ui > 1 else None
        sm0 = (ui % 2) * 64
        qA = sb_q[0:64, b, it * 128:(it + 1) * 128]
        qB = sb_q[64:128, b, it * 128:(it + 1) * 128]

        gl_blocks = [(L - 128, 128)] if L == 128 else [(L - 256, 0), (L - 128, 128)]

        unit_slot = {}
        for c in range(nch):
            j0 = c * CH
            s = min(CH, L - j0)
            dblocks = [(gc - j0, bc0) for (gc, bc0) in gl_blocks if gc // CH == c]
            qk_evs = {}
            for head in ("A", "B"):
                slot = (slot_ctr[0] % 3) * CH
                slot_ctr[0] += 1
                unit_slot[(c, head)] = slot
                w_slot = slot_user.get(slot)
                qh = qA if head == "A" else qB
                koff = 0 if head == "A" else 64
                tp = (0, 0) if head == "A" else (64, 0)
                dbanks = {}
                for (lc, bc0) in dblocks:
                    dbanks.setdefault((slot + lc) // 512, []).append((lc, bc0))
                spans = bank_spans(slot, s)
                n_d = len(dblocks)
                qk_ev = None
                for si, (off, n) in enumerate(spans):
                    has_d = (off // 512) in dbanks
                    is_last = (si == len(spans) - 1) and n_d == 0
                    def f(e, off=off, n=n, qh=qh, koff=koff, j0=j0, b=b, slot=slot,
                          has_d=has_d, tp=tp):
                        jrel = off - slot
                        return e.matmul(ps[:, off:off + n], qh,
                                        sb_k[koff:koff + 64, b, j0 + jrel: j0 + jrel + n],
                                        start=True, stop=not has_d, tile_position=tp)
                    w_ = []
                    if si == 0:
                        w_.append(w_slot)
                    if head == "A" and si == 0 and c == 0 and it == 0:
                        w_ += [ev_load[hp][0], ev_load[hp][1], ev_band_done]
                    qk_ev = P.op("tensor", f, waits=tuple(w_),
                                 inc=PE if is_last else None)
                di = 0
                for bkd, blist in sorted(dbanks.items()):
                    for (lc, bc0) in sorted(blist):
                        di += 1
                        last_in_bank = (lc == max(x for (x, _) in blist))
                        def f(e, slot=slot, lc=lc, bc0=bc0, hp=hp, koff=koff, last=last_in_bank):
                            h = 2 * hp + (0 if koff == 0 else 1)
                            return e.matmul(ps[:, slot + lc: slot + lc + 128],
                                            sb_identbf[:, :],
                                            sb_dband[:, h * 256 + bc0: h * 256 + bc0 + 128],
                                            start=False, stop=last)
                        qk_ev = P.op("tensor", f, waits=(),
                                     inc=PE if di == n_d else None)
                qk_evs[head] = qk_ev
            for head in ("A", "B"):
                if c != 0:
                    break
                qk_ev = qk_evs[head]
                rcol = sm0 + (0 if head == "A" else 32)
                slot = unit_slot[(0, head)]
                def fr(e, slot=slot, s=s, rcol=rcol):
                    return e.tensor_reduce(sb_sm[:, rcol:rcol + 1],
                                           ps[:, slot:slot + s], AX.X, ALU.max, negate=True)
                w_ = [qk_ev]
                if head == "A" and prev2:
                    w_.append(ev_resc_last[prev2])
                    w_.append(ev_exp_last[prev2])
                ev_m0[((hp, it), head)] = P.op("vector", fr, waits=tuple(w_), inc=VE)
            for head in ("A", "B"):
                qk_ev = qk_evs[head]
                rcol = sm0 + (0 if head == "A" else 32)
                slot = unit_slot[(c, head)]
                pbuf = sb_pA if head == "A" else sb_pB
                pslot = p_ctr[head][0] % 5
                p_ctr[head][0] += 1
                unit_pslot[((hp, it), c, head)] = pslot
                def fxp(e, slot=slot, s=s, pslot=pslot, rcol=rcol, c=c, pbuf=pbuf):
                    return e.activation(pbuf[:, pslot * CH:pslot * CH + s],
                                        ps[:, slot:slot + s], AF.Exp,
                                        bias=sb_sm[:, rcol:rcol + 1],
                                        accum_out=sb_sm[:, rcol + 12 + c:rcol + 13 + c])
                w_ = [qk_ev, ev_m0[((hp, it), head)], p_slot_user.get((head, pslot))]
                ev_x = P.op("scalar", fxp, waits=tuple(w_), inc=AC)
                ev_exp[(hp, it, c, head)] = ev_x
                slot_user[slot] = ev_x
            if c == 0 and prev is not None:
                emit_tail(prev)
        ev_exp_last[(hp, it)] = ev_exp[(hp, it, nch - 1, "B")]

        dg = None
        for head in ("A", "B"):
            rcol = sm0 + (0 if head == "A" else 32)
            ex = [ev_exp[(hp, it, c, head)] for c in range(nch)]
            if nch == 1:
                ev_d = P.op("vector", lambda e, rcol=rcol: e.reciprocal(
                    sb_sm[:, rcol + 31:rcol + 32], sb_sm[:, rcol + 12:rcol + 13]),
                    waits=(ex[0],), inc=VE)
            else:
                def tsa(dst, a, bcol, rcol=rcol):
                    def f(e, dst=dst, a=a, bcol=bcol, rcol=rcol):
                        return e.tensor_scalar_add(sb_sm[:, rcol + dst:rcol + dst + 1],
                                                   sb_sm[:, rcol + a:rcol + a + 1],
                                                   sb_sm[:, rcol + bcol:rcol + bcol + 1])
                    return f
                if nch == 2:
                    ev_f4 = P.op("gpsimd", tsa(30, 12, 13), waits=tuple(ex), inc=PO)
                elif nch == 3:
                    e22 = P.op("gpsimd", tsa(22, 12, 13), waits=tuple(ex[:2]), inc=PO)
                    ev_f4 = P.op("gpsimd", tsa(30, 14, 22), waits=(ex[2], e22), inc=PO)
                else:
                    e22 = P.op("gpsimd", tsa(22, 12, 13), waits=tuple(ex[:2]), inc=PO)
                    e23 = P.op("gpsimd", tsa(23, 14, 15), waits=tuple(ex[2:]), inc=PO)
                    ev_f4 = P.op("gpsimd", tsa(30, 22, 23), waits=(e22, e23), inc=PO)
                ev_d = P.op("vector", lambda e, rcol=rcol: e.reciprocal(
                    sb_sm[:, rcol + 31:rcol + 32], sb_sm[:, rcol + 30:rcol + 31]),
                    waits=(ev_f4,), inc=VE)
            pbuf = sb_pA if head == "A" else sb_pB
            for c in range(nch):
                s_c = min(CH, L - c * CH)
                pslot = unit_pslot[((hp, it), c, head)]
                def frs2(e, c=c, s_c=s_c, pslot=pslot, rcol=rcol, pbuf=pbuf):
                    return e.tensor_scalar_mul(pbuf[:, pslot * CH:pslot * CH + s_c],
                                               pbuf[:, pslot * CH:pslot * CH + s_c],
                                               sb_sm[:, rcol + 31:rcol + 32])
                dg = P.op("vector", frs2,
                          waits=(ev_d, ev_exp[(hp, it, c, head)]), inc=VE)
                ev_resc[((hp, it), c, head)] = dg
        ev_resc_last[(hp, it)] = dg
        if ui == len(units) - 1:
            emit_tail((hp, it))
    if debug:
        dma("sync", t_afm[:, :], sb_afm[:, :])
        dma("sync", t_dbd[:, :], sb_dband[:, :])
        dma("sync", t_dq0[:, :], sb_q[:, 0, :])
        dma("sync", t_dq1[:, :], sb_q[:, 1, :])
        dma("sync", t_dv0[:, :], sb_v[:, 0, :])
        dma("sync", t_dv1[:, :], sb_v[:, 1, :])
        dma("sync", t_dk0[:, :], sb_k[:, 0, :])
        dma("sync", t_dsm[:, :], sb_sm[:, :])
    ev_x1 = {}
    ev_add = {}
    xl2 = {}
    for it in range(2):
        xl2[it] = dma("sync", sb_xt[:, it % 2, :], t_x[it * 128:(it + 1) * 128, :],
                      waits=(ev_tr_last[(HP - 1, NT - 1)],))
    for it in range(NT):
        b = it % 2
        bank01 = (it % 2) * 2
        mm = None
        for hpc in range(8):
            for half in range(2):
                def f(e, it=it, hpc=hpc, half=half, bank01=bank01):
                    return e.matmul(ps[:, (bank01 + half) * 512:(bank01 + half + 1) * 512],
                                    sb_afm[:, hpc * T + it * 128: hpc * T + (it + 1) * 128],
                                    sb_woT[:, hpc * 1024 + half * 512: hpc * 1024 + half * 512 + 512],
                                    start=(hpc == 0), stop=False)
                w_ = []
                if hpc == 0 and half == 0:
                    w_.append(ev_add.get(it - 2))
                mm = P.op("tensor", f, waits=tuple(w_), inc=None)
        for half in range(2):
            def fb(e, half=half, bank01=bank01):
                return e.matmul(ps[:, (bank01 + half) * 512:(bank01 + half + 1) * 512],
                                sb_onesbf[0:1, :], sb_borow[0:1, half * 512:(half + 1) * 512],
                                start=False, stop=True)
            mm = P.op("tensor", fb, waits=(ev_boc,), inc=PE)
        def fx(e, b=b, bank01=bank01):
            return e.tensor_add(sb_x1st[:, b, :], ps[:, bank01 * 512:(bank01 + 2) * 512],
                                sb_xt[:, b, :])
        ev_add[it] = P.op("vector", fx, waits=(mm, xl2[it], ev_x1.get(it - 2)), inc=VE)
        ev_x1[it] = dma("sync", t_x1[it, :, :], sb_x1st[:, b, :], waits=(ev_add[it],))
        if it + 2 < NT:
            xl2[it + 2] = dma("sync", sb_xt[:, it % 2, :],
                              t_x[(it + 2) * 128:(it + 3) * 128, :],
                              waits=(ev_add[it],))

    def ffn_weight_prep(t_w, dstT, nrc, after=()):
        prev_ev = {}
        for rc in range(nrc):
            bi = rc % 2
            w_ = list(after) if rc == 0 else []
            w_.append(prev_ev.get(rc - 2))
            ld = dma("gpsimd", sb_istg[:, bi, :], t_w[rc * 128:(rc + 1) * 128, :],
                     waits=tuple(w_))
            cv = P.op("vector", lambda e, bi=bi: e.tensor_scalar_mul(
                sb_ibf[:, bi, :], sb_istg[:, bi, :], 1.0 / 127.0), waits=(ld,), inc=VE)
            tr = None
            for dc in range(8):
                def f(e, bi=bi, dc=dc, par=rc % 2, st=(dc == 0), sp=(dc == 7)):
                    return e.matmul(psbf(3072 + par * 512, 8)[:, dc * 128:(dc + 1) * 128],
                                    sb_ibf[:, bi, dc * 128:(dc + 1) * 128], sb_identbf[:, :],
                                    is_transpose=True, start=st, stop=sp)
                w2 = [cv]
                if dc == 0:
                    w2.append(prev_ev.get(rc - 2))
                tr = P.op("tensor", f, waits=tuple(w2), inc=PE if dc == 7 else None)
            def fe(e, rc=rc, par=rc % 2, dstT=dstT):
                full = dstT[:, :].rearrange("p (dc h) -> p dc h", dc=8)
                return e.tensor_copy(full[:, :, rc * 128:(rc + 1) * 128],
                                     psbf(3072 + par * 512, 8).rearrange("p (dc t) -> p dc t", dc=8))
            ev = P.op("vector", fe, waits=(tr,), inc=VE)
            prev_ev[rc] = ev
        return prev_ev[nrc - 1]

    ev_wg = ffn_weight_prep(t_wg, sb_wgT, 24, after=(ev_pv[(HP - 1, NT - 1)],))

    P.gate("g4")
    ev_wu = ffn_weight_prep(t_wu, sb_wuT, 24, after=())
    prev_wd = {}
    last_wd = None
    for rc2 in range(8):
        w_ = [ev_wu] if rc2 == 0 else [prev_wd.get(rc2 - 1)]
        ld = dma("gpsimd", sb_wdstg[:, :], t_wd[rc2 * 128:(rc2 + 1) * 128, :], waits=tuple(w_))
        evc = {}
        for chk in range(3):
            bi = chk % 2
            def fcv(e, chk=chk, bi=bi):
                return e.tensor_scalar_mul(sb_ibf[:, bi, :],
                                           sb_wdstg[:, chk * 1024:(chk + 1) * 1024], 1.0 / 127.0)
            w2 = [ld, evc.get(chk - 2)]
            if rc2 == 0 and chk < 2:
                w2.append(ev_wu)
            cv = P.op("vector", fcv, waits=tuple(w2), inc=VE)
            tr = None
            for q in range(8):
                def f(e, bi=bi, q=q, par=(rc2 + chk) % 2, st=(q == 0), sp=(q == 7)):
                    return e.matmul(psbf(3072 + par * 512, 8)[:, q * 128:(q + 1) * 128],
                                    sb_ibf[:, bi, q * 128:(q + 1) * 128], sb_identbf[:, :],
                                    is_transpose=True, start=st, stop=sp)
                w3 = [cv]
                if q == 0:
                    w3.append(prev_wd.get((rc2, chk - 2)) or prev_wd.get((rc2 - 1, chk + 1)))
                tr = P.op("tensor", f, waits=tuple(w3), inc=PE if q == 7 else None)
            def fe(e, rc2=rc2, chk=chk, par=(rc2 + chk) % 2):
                return e.tensor_copy(sb_wdstage[:, :], psbf(3072 + par * 512, 8))
            w_st = [tr]
            if (rc2, chk, "dma") in prev_wd:
                pass
            w_st.append(prev_wd.get("stage"))
            ev = P.op("vector", fe, waits=tuple(w_st), inc=VE)
            evd = None
            for q in range(8):
                hcq = chk * 8 + q
                evd = dma("sync", t_wdscr[hcq, :, rc2 * 128:(rc2 + 1) * 128],
                          sb_wdstage[:, q * 128:(q + 1) * 128], waits=(ev,))
            prev_wd["stage"] = evd
            evc[chk] = ev
            prev_wd[(rc2, chk)] = ev
        prev_wd[rc2] = evc[2]
        last_wd = prev_wd["stage"]

    ev_ffn = {}
    ev_odma = {}
    ev_fm = {}
    ev_tm2 = {}
    ev_var_prev = None
    ev_add3 = {}
    dwn_by_group = {}
    for ts in range(16):
        for k in range(2):
            tt = 2 * ts + k
            w_ = ([last_wd, ev_x1[NT - 1]] if ts == 0
                  else [ev_add3.get((ts - 1, k))])
            ld = dma("sync", sb_x1t[:, k, :], t_x1[tt, :, :], waits=tuple(w_))
            x1t = sb_x1t[:, k, :]
            ev_s1 = P.op("vector", lambda e, x1t=x1t: e.tensor_reduce(
                sb_ln[:, 0:1], x1t, AX.X, ALU.add), waits=(ld,), inc=VE)
            ev_s2 = P.op("scalar", lambda e, x1t=x1t: e.activation(
                sb_sq[:, :], x1t, AF.Square, accum_out=sb_ln[:, 1:2]),
                waits=(ld, ev_var_prev), inc=AC)
            ev_m = P.op("vector", lambda e: e.tensor_scalar_mul(sb_ln[:, 2:3], sb_ln[:, 0:1], LNC),
                        waits=(ev_s1,), inc=VE)
            ev_msq = P.op("vector", lambda e: e.tensor_scalar(
                sb_ln[:, 3:4], sb_ln[:, 2:3], sb_ln[:, 2:3], 1e-5,
                ALU.mult, ALU.subtract),
                waits=(ev_m,), inc=VE)
            ev_var = P.op("vector", lambda e: e.scalar_tensor_tensor(
                sb_ln[:, 4:5], sb_ln[:, 1:2], LNC, sb_ln[:, 3:4], ALU.mult, ALU.subtract),
                waits=(ev_s2, ev_msq), inc=VE)
            ev_var_prev = ev_var
            ev_lnv = P.op("scalar", lambda e: e.activation(
                sb_ln[:, 5:6], sb_ln[:, 4:5], AF.Ln), waits=(ev_var,), inc=AC)
            ev_rstd = P.op("scalar", lambda e: e.activation(
                sb_ln[:, 6:7], sb_ln[:, 5:6], AF.Exp, scale=-0.5),
                waits=(ev_lnv,), inc=AC)
            def ftm(e, k=k, x1t=x1t):
                return e.tensor_scalar(sb_xn1tm[:, k, :], x1t, sb_ln[:, 2:3], sb_ln[:, 6:7],
                                       ALU.subtract, ALU.mult)
            ev_tm2[(ts, k)] = P.op("vector", ftm, waits=(ev_rstd, ev_m, ev_tm2.get((ts - 1, k))), inc=VE)
            tr = None
            for dc in range(8):
                def f(e, k=k, dc=dc, par=k, st=(dc == 0), sp=(dc == 7)):
                    return e.matmul(psbf(3072 + par * 512, 8)[:, dc * 128:(dc + 1) * 128],
                                    sb_xn1tm[:, k, dc * 128:(dc + 1) * 128], sb_identbf[:, :],
                                    is_transpose=True, start=st, stop=sp)
                w2 = [ev_tm2[(ts, k)]]
                if dc == 0:
                    w2.append(ev_fm.get((ts - 1, k)))
                    w2.append(ev_add3.get((ts - 1, 1)))
                tr = P.op("tensor", f, waits=tuple(w2), inc=PE if dc == 7 else None)
            def fev(e, k=k, par=k):
                full = sb_xn1fm[:, :].rearrange("p (dc t) -> p dc t", t=256)
                return e.tensor_copy(full[:, :, k * 128:(k + 1) * 128],
                                     psbf(3072 + par * 512, 8).rearrange("p (dc t) -> p dc t", dc=8))
            ev_fm[(ts, k)] = P.op("vector", fev, waits=(tr,), inc=VE)

        hprev = {}
        down_last = None
        wd_ld = {}
        def emit_wd_load(hg, ts=ts):
            wb = hg % 2
            prev_use = dwn_by_group.get((ts, hg - 2))
            if prev_use is None and ts > 0:
                prev_use = dwn_by_group.get((ts - 1, hg + 4 - 6 + 2))
            if prev_use is None and ts > 0:
                prev_use = ev_ffn.get(ts - 1)
            wd_ld[hg] = dma("sync", sb_wdstr[:, wb, :],
                            t_wdscr[hg * 4:(hg + 1) * 4, :, :].rearrange("h p d -> p h d"),
                            waits=(prev_use, last_wd))
        for hg in range(2):
            emit_wd_load(hg)
        def emit_down(hh, ts=ts):
            bsl = hh % 2
            dlast = None
            for tsub in range(2):
                for half in range(2):
                    def fd(e, hh=hh, tsub=tsub, half=half, bsl=bsl,
                           first=(hh == 0), last=(hh == 23)):
                        c0 = 2048 + tsub * 1024 + half * 512
                        wb = (hh // 4) % 2
                        woff = (hh % 4) * 1024 + half * 512
                        return e.matmul(ps[:, c0:c0 + 512],
                                        sb_hsb[:, bsl, tsub * 128:(tsub + 1) * 128],
                                        sb_wdstr[:, wb, woff:woff + 512],
                                        start=first, stop=last)
                    w2 = [hprev[hh]]
                    if hh % 4 == 0 and tsub == 0 and half == 0:
                        w2.append(wd_ld[hh // 4])
                    if hh == 0 and tsub == 0 and half == 0 and ts >= 1:
                        w2.append(ev_add3.get((ts - 1, 0)))
                        w2.append(ev_add3.get((ts - 1, 1)))
                    dlast = P.op("tensor", fd, waits=tuple(w2),
                                 inc=PE if (tsub == 1 and half == 1) else None)
            if hh % 4 == 3:
                dwn_by_group[(ts, hh // 4)] = dlast
                if hh // 4 + 2 < 6:
                    emit_wd_load(hh // 4 + 2)
            return dlast
        for hh in range(24):
            gslot = (hh % 2) * 1024
            uslot = 512 + (hh % 2) * 1024
            bsl = hh % 2
            mmg = None
            mmu = None
            for dc in range(8):
                def fg(e, hh=hh, dc=dc, gslot=gslot):
                    return e.matmul(ps[:, gslot:gslot + 256],
                                    sb_wgT[:, dc * HID + hh * 128: dc * HID + hh * 128 + 128],
                                    sb_xn1fm[:, dc * 256:(dc + 1) * 256],
                                    start=(dc == 0), stop=(dc == 7))
                w2 = []
                if dc == 0:
                    w2 = [ev_fm[(ts, 0)], ev_fm[(ts, 1)], hprev.get(hh - 2)]
                mmg = P.op("tensor", fg, waits=tuple(w2), inc=PE if dc == 7 else None)
                def fu(e, hh=hh, dc=dc, uslot=uslot):
                    return e.matmul(ps[:, uslot:uslot + 256],
                                    sb_wuT[:, dc * HID + hh * 128: dc * HID + hh * 128 + 128],
                                    sb_xn1fm[:, dc * 256:(dc + 1) * 256],
                                    start=(dc == 0), stop=(dc == 7))
                mmu = P.op("tensor", fu, waits=(), inc=PE if dc == 7 else None)
            if hh >= 1:
                down_last = emit_down(hh - 1)
            def fs(e, gslot=gslot, bsl=bsl):
                return e.activation(sb_silu[:, bsl, :], ps[:, gslot:gslot + 256], AF.Silu)
            ev_si = P.op("scalar", fs, waits=(mmg, hprev.get(hh - 2)), inc=AC)
            def fm2(e, uslot=uslot, bsl=bsl):
                return e.tensor_mul(sb_hsb[:, bsl, :], ps[:, uslot:uslot + 256],
                                    sb_silu[:, bsl, :])
            ev_h = P.op("vector", fm2, waits=(ev_si, mmu), inc=VE)
            hprev[hh] = ev_h
            if debug and ts == 0 and hh == 0:
                dma("sync", t_xn1d[:, :], sb_xn1fm[:, :], waits=(ev_fm[(ts, 0)], ev_fm[(ts, 1)]))
                dma("sync", t_hd[:, :], sb_hsb[:, bsl, :], waits=(ev_h,))
                dma("sync", t_wsd[:, :], sb_wdstr[:, 0, :], waits=(wd_ld[0],))
        down_last = emit_down(23)
        evo = None
        for tsub in range(2):
            def fo2(e, tsub=tsub):
                return e.tensor_add(sb_ost[:, tsub, :],
                                    ps[:, 2048 + tsub * 1024: 2048 + (tsub + 1) * 1024],
                                    sb_x1t[:, tsub, :])
            w2 = [down_last, ev_odma.get((ts - 1, tsub))]
            ev_a = P.op("vector", fo2, waits=tuple(w2), inc=VE)
            ev_add3[(ts, tsub)] = ev_a
            tt = 2 * ts + tsub
            evo = dma("sync", t_out[tt * 128:(tt + 1) * 128, :], sb_ost[:, tsub, :],
                      waits=(ev_a,))
            ev_odma[(ts, tsub)] = evo
        ev_ffn[ts] = evo

    if debug:
        dma("sync", t_wgd[:, :], sb_wgT[:, :], waits=(ev_ffn[15],))
        dma("sync", t_wud[:, :], sb_wuT[:, :], waits=(ev_ffn[15],))

    final_ds = P.cnt[DS]
    P.op("sync", lambda e: e.nop(), waits=((DS, final_ds), (DG, P.cnt[DG])), inc=None)

    sems = {}
    import contextlib
    ctx = contextlib.ExitStack()
    for name in list(P.cnt.keys()):
        sems[name] = ctx.enter_context(nc.semaphore(name))
    with nc.Block() as block:
        P.emit(block, sems)
    ctx.close()
    return nc


_NC_CACHE = None


def _get_nc():
    global _NC_CACHE
    if _NC_CACHE is None:
        _NC_CACHE = build_nc()
    return _NC_CACHE


def kernel(L0_attn_x, L0_attn_Wq, L0_attn_bq, L0_attn_Wk, L0_attn_bk,
           L0_attn_Wv, L0_attn_bv, L0_attn_Wo, L0_attn_bo, L0_attn_rel_emb,
           L0_ffn_w_gate, L0_ffn_w_up, L0_ffn_w_down):
    nc = _get_nc()
    x = np.ascontiguousarray(np.asarray(L0_attn_x, dtype=np.float32))
    com = {
        "wq": np.ascontiguousarray(np.asarray(L0_attn_Wq, dtype=np.float32)),
        "wk": np.ascontiguousarray(np.asarray(L0_attn_Wk, dtype=np.float32)),
        "wv": np.ascontiguousarray(np.asarray(L0_attn_Wv, dtype=np.float32)),
        "wo": np.ascontiguousarray(np.asarray(L0_attn_Wo, dtype=np.float32)),
        "bq": np.ascontiguousarray(np.asarray(L0_attn_bq, dtype=np.float32).reshape(1, D)),
        "bk": np.ascontiguousarray(np.asarray(L0_attn_bk, dtype=np.float32).reshape(1, D)),
        "bv": np.ascontiguousarray(np.asarray(L0_attn_bv, dtype=np.float32).reshape(1, D)),
        "bo": np.ascontiguousarray(np.asarray(L0_attn_bo, dtype=np.float32).reshape(1, D)),
        "rel": np.ascontiguousarray(np.asarray(L0_attn_rel_emb, dtype=np.float32)),
        "wg": np.ascontiguousarray(np.asarray(L0_ffn_w_gate, dtype=np.int32)),
        "wu": np.ascontiguousarray(np.asarray(L0_ffn_w_up, dtype=np.int32)),
        "wd": np.ascontiguousarray(np.asarray(L0_ffn_w_down, dtype=np.int32)),
    }
    in_maps = []
    for c in range(8):
        m = dict(com)
        m["x"] = np.ascontiguousarray(x[c])
        in_maps.append(m)
    res = run_bass_kernel_spmd(nc, in_maps, core_ids=list(range(8)))
    out = np.stack([np.asarray(r["out"], dtype=np.float32) for r in res.results], axis=0)
    return out
